# version 23
# baseline (speedup 1.0000x reference)
import contextlib
import numpy as np
import concourse.bass as bass
import concourse.mybir as mybir
from concourse.bass_utils import run_bass_kernel_spmd

F32 = mybir.dt.float32
BF16 = mybir.dt.bfloat16
AF = mybir.ActivationFunctionType
ALU = mybir.AluOpType
AX = mybir.AxisListType
f32 = np.float32

T, D, H, BW, I, E = 2048, 1024, 8, 128, 4096, 4
NT, ND, NQ, QT = 16, 8, 4, 512
NI = I // 128
C1 = float(np.log1p(np.float64(-1e-5)))
A_MAX = float(np.float32(1.0 - 1e-5))
C7U = float(np.log(1e7) * (1.0 + 1e-6))
MIN_DECAY, EPS, SCAN_EPS, LN_EPS, LNS_EPS = 1e-3, 1e-5, 1e-7, 1e-12, 1e-5

ENGS = ("sync", "gpsimd", "vector", "scalar", "tensor")


class _Op:
    __slots__ = ("eng", "emit", "deps", "dma", "lane", "idx", "ev", "waits", "tag", "_key")


class Sched:

    def __init__(self, elide_same_engine=()):
        self.elide = set(elide_same_engine)
        self.ops = []
        self.lw = {}
        self.rd = {}
        self.global_dep = None

    def add(self, eng, emit, reads=(), writes=(), dma=False, deps=(), tag="",
            lane=None):
        op = _Op()
        op.eng, op.emit, op.dma, op.tag = eng, emit, dma, tag
        op.lane = lane
        op.idx = len(self.ops)
        d = set()
        for x in deps:
            if x is not None:
                d.add(x)
        if self.global_dep is not None:
            d.add(self.global_dep)
        for r in reads:
            w = self.lw.get(r)
            if w is not None:
                d.add(w)
        for w_ in writes:
            w = self.lw.get(w_)
            if w is not None:
                d.add(w)
            for r in self.rd.get(w_, ()):
                d.add(r)
        d.discard(op)
        op.deps = d
        for r in reads:
            self.rd.setdefault(r, []).append(op)
        for w_ in writes:
            self.lw[w_] = op
            self.rd[w_] = []
        self.ops.append(op)
        return op

    def fence(self, eng="vector", tag="fence"):
        deps = set(self.lw.values())
        for lst in self.rd.values():
            deps.update(lst)
        op = self.add(eng, lambda e: e.engine_nop(), deps=deps, tag=tag)
        self.global_dep = op
        self.lw = {}
        self.rd = {}
        return op

    def finalize(self, nc, es):
        if self.elide:
            for op in self.ops:
                op.deps = {
                    d for d in op.deps
                    if not (d.eng == op.eng and op.eng in self.elide
                            and not d.dma and not op.dma)
                }
        keys = []
        for op in self.ops:
            op.ev = None
            k = ("D", op.eng, op.lane) if op.dma else ("C", op.eng)
            if k not in keys:
                keys.append(k)
            op._key = k
        def fam(op):
            return (op.eng, op.lane.rstrip("0123456789")) if op.dma else None
        for op in self.ops:
            if op.dma:
                op.deps = {d for d in op.deps
                           if not (d.dma and fam(d) == fam(op))}
        for op in self.ops:
            if len(op.deps) > 1:
                best = {}
                keep = set()
                for d in op.deps:
                    if d.dma:
                        keep.add(d)
                    else:
                        cur = best.get(d.eng)
                        if cur is None or d.idx > cur.idx:
                            best[d.eng] = d
                op.deps = keep | set(best.values())
        self.sems = {}
        for i, k in enumerate(keys):
            nm = "_".join(str(p) for p in k)
            self.sems[k] = es.enter_context(nc.semaphore(f"s{i}_{nm}"))
        needed = set()
        for op in self.ops:
            needed.update(op.deps)
        ctr = {k: 0 for k in keys}
        for op in self.ops:
            if op in needed or op.dma:
                k = op._key
                ctr[k] += 16 if op.dma else 1
                op.ev = (k, ctr[k])
        op_vc = {}
        engine_vc = {e: {} for e in ENGS}
        for op in self.ops:
            vc = {}
            for d in op.deps:
                for k, v in op_vc[d].items():
                    if vc.get(k, 0) < v:
                        vc[k] = v
            evc = engine_vc[op.eng]
            wmax = {}
            for d in op.deps:
                k, v = d.ev
                if evc.get(k, 0) >= v:
                    continue
                if wmax.get(k, 0) < v:
                    wmax[k] = v
                for k2, v2 in op_vc[d].items():
                    if evc.get(k2, 0) < v2:
                        evc[k2] = v2
            op.waits = sorted(wmax.items(), key=lambda kv: str(kv[0]))
            if op.ev is not None:
                k, v = op.ev
                if vc.get(k, 0) < v:
                    vc[k] = v
            for k, v in evc.items():
                if vc.get(k, 0) < v:
                    vc[k] = v
            op_vc[op] = vc
        fin = self.ops[-1]
        assert fin.tag == "final_wait"
        evc = engine_vc[fin.eng]
        have = dict(fin.waits)
        for k in keys:
            if k[0] == "D" and ctr[k] > max(evc.get(k, 0), have.get(k, 0)):
                have[k] = ctr[k]
        fin.waits = sorted(have.items(), key=lambda kv: str(kv[0]))

    def replay(self, engine, eng):
        for op in self.ops:
            if op.eng != engine:
                continue
            for k, v in op.waits:
                eng.wait_ge(self.sems[k], v)
            inst = op.emit(eng)
            if op.ev is not None:
                inst.then_inc(self.sems[op.ev[0]], 16 if op.dma else 1)


def build(dbg=False, stage=99, elide=("tensor",)):
    nc = bass.Bass("TRN2", target_bir_lowering=False)
    es = contextlib.ExitStack()
    S = Sched(elide_same_engine=elide or ())

    dr = lambda name, shape, dt, kind: nc.dram_tensor(name, shape, dt, kind=kind).ap()
    x_h = dr("x", [T, D], F32, "ExternalInput")
    ap_h = dr("a_param", [D], F32, "ExternalInput")
    win_h = dr("w_in", [H, BW, BW], F32, "ExternalInput")
    bin_h = dr("b_in", [H, BW], F32, "ExternalInput")
    wa_h = dr("w_a", [H, BW, BW], F32, "ExternalInput")
    ba_h = dr("b_a", [H, BW], F32, "ExternalInput")
    wr_h = dr("w_router", [D, E], F32, "ExternalInput")
    w1_h = dr("w1", [E, D, I], F32, "ExternalInput")
    b1_h = dr("b1", [E, I], F32, "ExternalInput")
    wo_h = dr("wo", [I, D], F32, "ExternalInput")
    bo_h = dr("bo", [D], F32, "ExternalInput")
    lng_h = dr("ln_gamma", [D], F32, "ExternalInput")
    lnb_h = dr("ln_beta", [D], F32, "ExternalInput")
    out_h = dr("out", [T, D], F32, "ExternalOutput")

    scr_kind = "ExternalOutput" if dbg else "Internal"
    r2_scr = dr("r2_scr", [T, D], F32, scr_kind)
    x2_scr = dr("x2_scr", [T, D], F32, scr_kind)
    x2fm_scr = dr("x2fm_scr", [D, T], BF16, "Internal")
    w1b_scr = dr("w1b_scr", [NI // 2, 128, E * ND * 256], BF16, "Internal")
    wob_scr = dr("wob_scr", [I, D], BF16, "Internal")
    vec_scr = dr("vec_scr", [D], F32, "Internal")
    if dbg:
        dbg_h = dr("d_h", [D, T], F32, "ExternalOutput")
        dbg_lg = dr("d_lg", [T, E], F32, "ExternalOutput")
        dbg_m = dr("d_m", [T, E], F32, "ExternalOutput")
        dbg_u2 = dr("d_u2", [T, D], F32, "ExternalOutput")
        dbg_x1 = dr("d_x1", [T, D], F32, "ExternalOutput")

    A_per = nc.auto_arena()
    A_attn = nc.auto_arena()
    A_ffn = nc.auto_arena()

    class _W:
        def __init__(self, arena):
            self.arena = arena
        def t(self, name, shape, dtype):
            return self.arena.sbuf_tensor(name, shape, dtype)
    A = _W(A_per)
    identf = A.t("identf", [128, 128], F32)
    iot = A.t("iot", [128, 128], F32)
    onesb = A.t("onesb", [1, 128], BF16)
    onesf = A.t("onesf", [1, 128], F32)
    w_inb = A.t("w_inb", [128, H * 128], F32)
    w_ab = A.t("w_ab", [128, H * 128], F32)
    hbin = A.t("hbin", [128, H], F32)
    hba = A.t("hba", [128, H], F32)
    nhc = A.t("nhc", [128, ND], F32)
    wrf = A.t("wrf", [128, ND * E], F32)
    m2r = [A.t(f"m2r{e}", [1, T], BF16) for e in range(E)]
    m4r = A.t("m4r", [4, T], BF16)
    b1b4 = A.t("b1b4", [4, I], BF16)
    mcols = A.t("mcols", [128, NT * E], F32)
    gam_b = A.t("gam_b", [128, D], F32)
    bet_b = A.t("bet_b", [128, D], F32)
    bo_b = A.t("bo_b", [128, D], F32)
    carryA = A.t("carryA", [128, ND], F32)
    carryB = A.t("carryB", [128, ND], F32)
    scol = [A.t(f"scol{i}", [128, 16], F32) for i in range(8)]
    m4b_ = [A.t(f"m4b{i}", [4, 128], BF16) for i in range(2)]
    m4f_ = [A.t(f"m4f{i}", [4, 128], F32) for i in range(2)]
    lgt_ = [A.t(f"lgt{i}", [128, 8], F32) for i in range(2)]
    eqt_ = [A.t(f"eqt{i}", [128, 4], F32) for i in range(2)]
    cum_ = [A.t(f"cum{i}", [128, 2], F32) for i in range(2)]

    A = _W(A_attn)
    x_t = [A.t(f"x{i}", [128, D], F32) for i in range(4)]
    xr_t = [A.t(f"xr{i}", [128, D], F32) for i in range(2)]
    x1tok = [A.t(f"x1tok{i}", [128, D], F32) for i in range(2)]
    x1fm_ = [A.t(f"x1fm{i}", [128, ND * QT], F32) for i in range(2)]
    trp = {k: [A.t(f"tr_{k}{i}", [128, QT], F32) for i in range(3)]
           for k in ("tga", "tgxr", "asqS", "nrm", "ce", "p")}
    h_fm = A.t("h_fm", [128, ND * QT], F32)
    htok_ = [A.t(f"htok{i}", [128, D], F32) for i in range(2)]
    r2t_ = [A.t(f"r2t{i}", [128, D], F32) for i in range(2)]
    x2t_ = [A.t(f"x2t{i}", [128, D], F32) for i in range(2)]
    ln2h_ = x2t_
    x2fmT = [A.t(f"x2fmT{i}", [128, ND * 128], F32) for i in range(2)]
    x2fmTb = [A.t(f"x2fmTb{i}", [128, ND * 128], BF16) for i in range(2)]
    rowbuf = x2fmT
    braw = A.t("braw", [8, 128], F32)

    A = _W(A_ffn)
    x2fq = A.t("x2fq", [128, ND * QT], BF16)
    maskb = [A.t(f"maskb{e}", [128, QT], BF16) for e in range(E)]
    x2fe = [A.t(f"x2fe{e}", [128, ND * QT], BF16) for e in range(E)]
    interm = A.t("interm", [128, NI * QT], BF16)
    w1bb = [A.t(f"w1bb{i}", [128, E * ND * 256], BF16) for i in range(2)]
    wobb = [A.t(f"wobb{i}", [128, D], BF16) for i in range(2)]
    x2rel = [A.t(f"x2rel{i}", [128, D], F32) for i in range(2)]
    r2rel = [A.t(f"r2rel{i}", [128, D], F32) for i in range(2)]
    ubuf = A.t("ubuf", [128, D], F32)
    y2buf = A.t("y2buf", [128, D], F32)

    ps = [es.enter_context(nc.psum_tensor(f"ps{i}", [128, 1024], F32))
          for i in range(4)]

    def dve(emit, reads=(), writes=(), tag="", lane=None):
        return S.add("vector", emit, reads, writes, tag=tag)

    def act(emit, reads=(), writes=(), tag="", lane=None):
        return S.add("scalar", emit, reads, writes, tag=tag)

    def pool(emit, reads=(), writes=(), tag="", lane=None):
        return S.add("gpsimd", emit, reads, writes, tag=tag)

    def pe(emit, reads=(), writes=(), tag="", lane=None):
        return S.add("tensor", emit, reads, writes, tag=tag)

    def dma(q, emit, reads=(), writes=(), lane="misc", tag=""):
        return S.add(q, emit, reads, writes, dma=True, tag=tag, lane=lane)

    for j in range(4):
        dma("sync", lambda e, j=j: e.dma_start(
            out=x_t[j][:], in_=x_h[j * 128:(j + 1) * 128, :]),
            writes=[f"x{j}"], lane=f"x{j}", tag="xload")
    for e_ in range(E):
        dma("gpsimd", lambda e, e_=e_: e.dma_start(
            out=b1b4[e_:e_ + 1, :], in_=b1_h[e_][None, :]),
            writes=["b1b4"], lane="b1b", tag="b1ld")

    pool(lambda e: e.iota(iot[:], pattern=[[1, 128]], base=0, channel_multiplier=-1,
                          allow_small_or_imprecise_dtypes=True),
         writes=["iot"], tag="iota")
    dve(lambda e: e.tensor_single_scalar(out=identf[:], in_=iot[:], scalar=0.0,
                                         op=ALU.is_equal),
        reads=["iot"], writes=["identf"])
    dve(lambda e: e.memset(onesf[:], 1.0), writes=["onesf"])
    dve(lambda e: e.tensor_copy(onesb[:], onesf[:]), reads=["onesf"], writes=["onesb"])

    for h in range(H):
        dma("gpsimd", lambda e, h=h: e.dma_start(
            out=w_inb[:, h * 128:(h + 1) * 128], in_=win_h[h]),
            writes=["w_inb"], lane="pregp", tag="w_inb")
        dma("gpsimd", lambda e, h=h: e.dma_start(
            out=w_ab[:, h * 128:(h + 1) * 128], in_=wa_h[h]),
            writes=["w_ab"], lane="pregp", tag="w_ab")


    for name, hsrc, dstc, rdst in (("bi", bin_h, hbin, "hbin"),
                                   ("ba", ba_h, hba, "hba")):
        dma("sync", lambda e, hsrc=hsrc: e.dma_start(out=braw[0:8, 0:128], in_=hsrc),
            reads=["braw"], writes=["braw"], lane="braw", tag="braw")
        pe(lambda e: e.transpose(out=ps[3][0:128, 0:8], in_=braw[0:8, 0:128],
                                 identity=identf[0:8, 0:8]),
           reads=["braw", "identf"], writes=["ps3a"], tag="btr")
        dve(lambda e, dstc=dstc: e.tensor_scalar_mul(
            out=dstc[:], in0=ps[3][0:128, 0:8], scalar1=0.5),
            reads=["ps3a"], writes=[rdst], tag="bev")

    r0, r1 = rowbuf[0], rowbuf[1]
    dma("sync", lambda e: e.dma_start(out=r0[0:1, 0:D], in_=ap_h[None, :]),
        writes=["r0"], lane="apr")
    ev_abs = act(lambda e: e.activation(out=r1[0:1, 0:D], in_=r0[0:1, 0:D], func=AF.Abs),
                 reads=["r0"], writes=["r1"])
    act(lambda e: e.activation(out=r1[0:1, 0:D], in_=r1[0:1, 0:D], func=AF.Exp,
                               scale=f32(-1.0)), reads=["r1"], writes=["r1"])
    act(lambda e: e.activation(out=r1[0:1, 0:D], in_=r1[0:1, 0:D], func=AF.Ln,
                               bias=f32(1.0)), reads=["r1"], writes=["r1"])
    dve(lambda e: e.tensor_scalar_max(out=r0[0:1, 0:D], in0=r0[0:1, 0:D], scalar1=0.0),
        reads=["r0"], writes=["r0"], lane="apr")
    dve(lambda e: e.tensor_add(out=r0[0:1, 0:D], in0=r0[0:1, 0:D], in1=r1[0:1, 0:D]),
        reads=["r0", "r1"], writes=["r0"], lane="apr")
    dve(lambda e: e.tensor_scalar(out=r0[0:1, 0:D], in0=r0[0:1, 0:D],
                                  scalar1=MIN_DECAY, scalar2=-0.5,
                                  op0=ALU.add, op1=ALU.mult),
        reads=["r0"], writes=["r0"], lane="apr")
    dma("sync", lambda e: e.dma_start(out=vec_scr[None, :], in_=r0[0:1, 0:D]),
        reads=["r0"], writes=["vec_scr"], lane="vscs")
    dma("sync", lambda e: e.dma_start(
        out=braw[0:8, 0:128], in_=vec_scr.rearrange("(c p) -> c p", p=128)),
        reads=["vec_scr", "braw"], writes=["braw"], lane="vscl")
    pe(lambda e: e.transpose(out=ps[3][0:128, 0:8], in_=braw[0:8, 0:128],
                             identity=identf[0:8, 0:8]),
       reads=["braw", "identf"], writes=["ps3a"], tag="nhctr")
    dve(lambda e: e.tensor_copy(nhc[:], ps[3][0:128, 0:8]),
        reads=["ps3a"], writes=["nhc"], tag="nhcev")

    with nc.allow_non_contiguous_dma(reason="tiny router weight load"):
        wr_r = wr_h.rearrange("(k p) e -> p k e", p=128)
        wrf_r = wrf.ap().rearrange("p (k e) -> p k e", k=ND)
        dma("scalar", lambda e: e.dma_start(out=wrf_r, in_=wr_r,
                                            allow_slow_non_contiguous=True),
            writes=["wrf"], lane="wrf")

    for name, src, dst in (("g", lng_h, gam_b), ("b", lnb_h, bet_b), ("o", bo_h, bo_b)):
        dma("sync", lambda e, src=src: e.dma_start(out=r1[0:1, 0:D], in_=src[None, :]),
            reads=["r1"], writes=["r1"], lane="brow", tag="bcast_row")
        for half in range(2):
            pe(lambda e, half=half: e.matmul(
                out=ps[3][:, half * 512:(half + 1) * 512], lhsT=onesf[0:1, :],
                rhs=r1[0:1, half * 512:(half + 1) * 512], start=True, stop=True),
               reads=["r1", "onesf"], writes=[f"ps3{'ab'[half]}"], tag="bcast_mm")
        dve(lambda e, dst=dst: e.tensor_copy(dst[:], ps[3][:]),
            reads=["ps3a", "ps3b"], writes=[name + "_b"])

    dve(lambda e: e.memset(carryA[:], 0.0), writes=["carryA"])
    dve(lambda e: e.memset(carryB[:], 0.0), writes=["carryB"])

    _pre_res = ["identf", "identb", "onesb", "onesf", "w_inb", "w_ab", "hbin",
                "hba", "nhc", "g_b", "b_b", "o_b", "carryA", "carryB"]
    dve(lambda e: e.engine_nop(), reads=list(_pre_res), writes=list(_pre_res),
        tag="prelude_barrier")

    for e_ in range(E):
        for k in range(ND):
            col = (e_ * ND + k) * 256
            for half in range(2):
                dma("gpsimd", lambda e, e_=e_, k=k, col=col, half=half:
                    e.dma_start(
                        out=w1b_scr[half * 8:(half + 1) * 8, :, col:col + 256],
                        in_=w1_h[e_, k * 128:(k + 1) * 128,
                                 half * 2048:(half + 1) * 2048].rearrange(
                                     "p (ig i) -> ig p i", i=256)),
                    writes=["w1scr"], lane="cast", tag="w1cast")
    for k in range(NI):
        dma("gpsimd", lambda e, k=k: e.dma_start(
            out=wob_scr[k * 128:(k + 1) * 128, :],
            in_=wo_h[k * 128:(k + 1) * 128, :]),
            writes=["woscr"], lane="cast", tag="wocast")


    def layernorm(src_t, rs, dst_t, rd, eps, ci, tag, defer=None):
        sc = scol[ci]
        rc = f"scol{ci}"
        o = []
        o.append(lambda: dve(lambda e: e.bn_stats(out=sc[:, 0:6],
                                                  in_=src_t[:, 0:512]),
                             reads=[rs], writes=[rc], tag=tag + "_bst"))
        o.append(lambda: dve(lambda e: e.bn_stats(out=sc[:, 6:12],
                                                  in_=src_t[:, 512:1024]),
                             reads=[rs], writes=[rc], tag=tag + "_bs2"))
        o.append(lambda: dve(lambda e: e.bn_aggr(out=sc[:, 12:14], in_=sc[:, 0:12]),
                             reads=[rc], writes=[rc], tag=tag + "_bag"))
        o.append(lambda: act(lambda e: e.activation(
            out=sc[:, 14:15], in_=sc[:, 13:14], func=AF.Ln, bias=f32(eps)),
            reads=[rc], writes=[rc], tag=tag + "_ln"))
        o.append(lambda: act(lambda e: e.activation(
            out=sc[:, 14:15], in_=sc[:, 14:15], func=AF.Exp, scale=f32(-0.5)),
            reads=[rc], writes=[rc], tag=tag + "_ex"))
        o.append(lambda: dve(lambda e: e.scalar_tensor_tensor(
            out=sc[:, 15:16], in0=sc[:, 12:13], scalar=-1.0, in1=sc[:, 14:15],
            op0=ALU.mult, op1=ALU.mult),
            reads=[rc], writes=[rc]))
        o.append(lambda: act(lambda e: e.activation(
            out=dst_t[:], in_=src_t[:], func=AF.Identity,
            scale=sc[:, 14:15], bias=sc[:, 15:16]),
            reads=[rs, rc], writes=[rd], tag=tag + "_ap"))
        if defer is None:
            for f in o:
                f()
        else:
            defer.extend(o)


    def emit_A1(q):
        if q >= 1:
            for jj in range(4):
                j = q * 4 + jj
                dma("sync", lambda e, j=j: e.dma_start(
                    out=x_t[j % 4][:], in_=x_h[j * 128:(j + 1) * 128, :]),
                    writes=[f"x{j % 4}"], lane=f"x{j % 4}", tag="xload")
        def a1_ops(jj):
            j = q * 4 + jj
            par = j % 2
            xb = x_t[j % 4]
            rx = f"x{j % 4}"
            x1 = x1tok[par]
            rx1 = f"x1tok{par}"
            o = []
            layernorm(xb, rx, x1, rx1, LNS_EPS, 0 + 4 * par, f"ln1_{j}", defer=o)
            if stage < 1.6:
                return o
            if dbg:
                o.append(lambda j=j, x1=x1, rx1=rx1: dma(
                    "scalar", lambda e: e.dma_start(
                        out=dbg_x1[j * 128:(j + 1) * 128, :], in_=x1[:]),
                    reads=[rx1], lane=f"dbg{j}", tag="dbg_x1"))
            for k in range(ND):
                o.append(lambda k=k, x1=x1, rx1=rx1, par=par: pe(
                    lambda e: e.transpose(
                        out=ps[2 + par][:, k * 128:(k + 1) * 128],
                        in_=x1[:, k * 128:(k + 1) * 128], identity=identf[:]),
                    reads=[rx1, "identf"],
                    writes=[f"ps{2 + par}a" if k < 4 else f"ps{2 + par}b"],
                    tag="x1tr"))
            if stage < 1.9:
                return o
            x1fm = x1fm_[q % 2]
            rx1fm = f"x1fm{q % 2}"
            dst = x1fm.ap().rearrange("p (k t) -> p k t", k=ND)[
                :, :, jj * 128:(jj + 1) * 128]
            srcp = ps[2 + par].ap().rearrange("p (k t) -> p k t", k=ND)
            o.append(lambda dst=dst, srcp=srcp, par=par, rx1fm=rx1fm: dve(
                lambda e: e.tensor_copy(dst, srcp),
                reads=[f"ps{2 + par}a", f"ps{2 + par}b"], writes=[rx1fm],
                tag="x1ev"))
            return o

        for jp in range(0, 4, 2):
            oa, ob = a1_ops(jp), a1_ops(jp + 1)
            for i in range(max(len(oa), len(ob))):
                if i < len(oa):
                    oa[i]()
                if i < len(ob):
                    ob[i]()

        return

    def emit_chunks(q):
        def chunk_ops(k):
            x1fm = x1fm_[q % 2]
            rx1fm = f"x1fm{q % 2}"
            par = k % 3
            sfx = str(par)
            tga = trp["tga"][par]
            tgx = trp["tgxr"][par]
            r_ = tgx
            asq = trp["asqS"][par]
            lnm = asq
            S_ = asq
            nrm, ce, p_ = trp["nrm"][par], trp["ce"][par], trp["p"][par]
            rtga = "tga" + sfx
            rtgx = rr = "tgxr" + sfx
            rasq = rlnm = rS = "asqS" + sfx
            rnrm, rce, rp = "nrm" + sfx, "ce" + sfx, "p" + sfx
            rcA, rcB = f"carryA{k}", f"carryB{k}"
            gb = ps[[2, 3, 0][par]]
            gbn = [2, 3, 0][par]
            o = []
            for gi, (wb, rw, hb, rh, tt, rtt) in enumerate(
                    ((w_inb, "w_inb", hbin, "hbin", tgx, rtgx),
                     (w_ab, "w_ab", hba, "hba", tga, rtga))):
                bank = f"ps{gbn}{'ab'[gi]}"
                off = gi * 512
                o.append(lambda k=k, wb=wb, off=off, gb=gb, rw=rw, bank=bank: pe(
                    lambda e: e.matmul(
                        out=gb[:, off:off + 512], lhsT=wb[:, k * 128:(k + 1) * 128],
                        rhs=x1fm[:, k * QT:(k + 1) * QT], start=True, stop=True),
                    reads=[rw, rx1fm], writes=[bank], tag="gmm"))
                o.append(lambda k=k, hb=hb, off=off, gb=gb, tt=tt, rh=rh,
                         bank=bank, rtt=rtt: act(
                    lambda e: e.activation(
                        out=tt[:], in_=gb[:, off:off + 512], func=AF.Tanh,
                        scale=f32(0.5), bias=hb[:, k:k + 1]),
                    reads=[bank, rh], writes=[rtt], tag="tanh"))
            o.append(lambda k=k: dve(lambda e: e.tensor_scalar(
                out=tga[:], in0=tga[:], scalar1=nhc[:, k:k + 1],
                scalar2=nhc[:, k:k + 1], op0=ALU.mult, op1=ALU.add),
                reads=[rtga, "nhc"], writes=[rtga], tag="la"))
            o.append(lambda: dve(lambda e: e.tensor_scalar_min(
                out=tga[:], in0=tga[:], scalar1=C1),
                reads=[rtga], writes=[rtga]))
            o.append(lambda: act(lambda e: e.activation(
                out=asq[:], in_=tga[:], func=AF.Exp, scale=f32(2.0)),
                reads=[rtga], writes=[rasq], tag="expa"))
            o.append(lambda: dve(lambda e: e.tensor_scalar_min(
                out=asq[:], in0=asq[:], scalar1=A_MAX * A_MAX),
                reads=[rasq], writes=[rasq]))
            o.append(lambda: dve(lambda e: e.tensor_scalar(
                out=asq[:], in0=asq[:], scalar1=-1.0, scalar2=1.0,
                op0=ALU.mult, op1=ALU.add),
                reads=[rasq], writes=[rasq]))
            o.append(lambda: act(lambda e: e.activation(
                out=lnm[:], in_=asq[:], func=AF.Ln),
                reads=[rasq], writes=[rlnm], tag="lnmul"))
            o.append(lambda: act(lambda e: e.activation(
                out=lnm[:], in_=lnm[:], func=AF.Exp, scale=f32(0.5),
                bias=f32(float(np.log(0.5)))),
                reads=[rlnm], writes=[rlnm], tag="expmul"))
            o.append(lambda k=k: dve(lambda e: e.tensor_mul(
                out=nrm[:], in0=x1fm[:, k * QT:(k + 1) * QT], in1=lnm[:]),
                reads=[rx1fm, rlnm], writes=[rnrm]))
            o.append(lambda: dve(lambda e: e.scalar_tensor_tensor(
                out=nrm[:], in0=tgx[:], scalar=1.0, in1=nrm[:],
                op0=ALU.add, op1=ALU.mult),
                reads=[rtgx, rnrm], writes=[rnrm]))
            o.append(lambda k=k: dve(lambda e: e.tensor_copy(
                ce[:, 0:1], carryA[:, k:k + 1]),
                reads=[rcA], writes=[rce], tag="ce0"))
            o.append(lambda k=k: dve(lambda e: e.tensor_tensor_scan(
                out=ce[:, 1:QT], data0=tga[:, 0:QT - 1], data1=tga[:, 0:QT - 1],
                initial=carryA[:, k:k + 1], op0=ALU.add, op1=ALU.bypass),
                reads=[rtga, rcA], writes=[rce], tag="scanA"))
            o.append(lambda k=k: dve(lambda e: e.tensor_add(
                out=carryA[:, k:k + 1], in0=ce[:, QT - 1:QT], in1=tga[:, QT - 1:QT]),
                reads=[rce, rtga], writes=[rcA], tag="cAup"))
            o.append(lambda: act(lambda e: e.activation(
                out=p_[:], in_=ce[:], func=AF.Exp),
                reads=[rce], writes=[rp], tag="expp"))
            o.append(lambda: dve(lambda e: e.tensor_scalar_max(
                out=p_[:], in0=p_[:], scalar1=SCAN_EPS),
                reads=[rp], writes=[rp]))
            o.append(lambda: dve(lambda e: e.tensor_scalar_max(
                out=ce[:], in0=ce[:], scalar1=-C7U),
                reads=[rce], writes=[rce], tag="cecl"))
            o.append(lambda: act(lambda e: e.activation(
                out=r_[:], in_=ce[:], func=AF.Exp, scale=f32(-1.0)),
                reads=[rce], writes=[rr], tag="expr"))
            o.append(lambda: dve(lambda e: e.tensor_scalar_min(
                out=r_[:], in0=r_[:], scalar1=1.0 / SCAN_EPS),
                reads=[rr], writes=[rr]))
            o.append(lambda: dve(lambda e: e.scalar_tensor_tensor(
                out=nrm[:], in0=nrm[:], scalar=SCAN_EPS, in1=r_[:],
                op0=ALU.add, op1=ALU.mult),
                reads=[rnrm, rr], writes=[rnrm], tag="w"))
            o.append(lambda k=k: dve(lambda e: e.tensor_tensor_scan(
                out=S_[:], data0=nrm[:], data1=nrm[:],
                initial=carryB[:, k:k + 1], op0=ALU.add, op1=ALU.bypass),
                reads=[rnrm, rcB], writes=[rS], tag="scanB"))
            o.append(lambda k=k: dve(lambda e: e.tensor_copy(
                carryB[:, k:k + 1], S_[:, QT - 1:QT]),
                reads=[rS], writes=[rcB], tag="cBup"))
            o.append(lambda k=k: dve(lambda e: e.tensor_mul(
                out=h_fm[:, k * QT:(k + 1) * QT], in0=p_[:], in1=S_[:]),
                reads=[rp, rS], writes=["h_fm"], tag="h"))
            return o

        if stage >= 3:
            for kp in (0, 3, 6):
                grp = [chunk_ops(kp + i) for i in range(min(3, ND - kp))]
                for i in range(max(len(g) for g in grp)):
                    for g in grp:
                        if i < len(g):
                            g[i]()

        if dbg:
            dma("scalar", lambda e, q=q: e.dma_start(
                out=dbg_h.rearrange("(k p) t -> p k t", p=128)[
                    :, :, q * QT:(q + 1) * QT],
                in_=h_fm.ap().rearrange("p (k t) -> p k t", k=ND)),
                reads=["h_fm"], lane=f"dbg{q * 4}", tag="dbg_h")

        return

    def emit_A4(q):
        def a4_ops(jj):
            j = q * 4 + jj
            par = j % 2
            htok, ln2h = htok_[par], ln2h_[par]
            r2t, x2t = r2t_[par], x2t_[par]
            lgt, eqt, cum = lgt_[par], eqt_[par], cum_[par]
            rhtok, rln2h = f"htok{par}", f"x2t{par}"
            rr2t, rx2t = f"r2t{par}", f"x2t{par}"
            rlgt, reqt, rcum = f"lgt{par}", f"eqt{par}", f"cum{par}"
            lps = ps[par]
            rlga = f"ps{par}a"
            rlgb = f"ps{par}b"
            xb = xr_t[par]
            rx = f"xr{par}"
            o = []
            o.append(lambda j=j, xb=xb, rx=rx: dma(
                "sync", lambda e: e.dma_start(
                    out=xb[:], in_=x_h[j * 128:(j + 1) * 128, :]),
                reads=[rx], writes=[rx], lane=f"xr{par}", tag="xrload"))
            for k in range(ND):
                o.append(lambda k=k, jj=jj, par=par: pe(
                    lambda e: e.transpose(
                        out=ps[par][:, k * 128:(k + 1) * 128],
                        in_=h_fm[:, k * QT + jj * 128: k * QT + (jj + 1) * 128],
                        identity=identf[:]),
                    reads=["h_fm", "identf"],
                    writes=[f"ps{par}a" if k < 4 else f"ps{par}b"], tag="htr"))
            o.append(lambda par=par, htok=htok, rhtok=rhtok: dve(
                lambda e: e.tensor_copy(htok[:], ps[par][:]),
                reads=[f"ps{par}a", f"ps{par}b"], writes=[rhtok], tag="hev"))
            layernorm(htok, rhtok, ln2h, rln2h, LNS_EPS, 1 + 4 * par,
                      f"ln2_{j}", defer=o)
            o.append(lambda xb=xb, rx=rx, r2t=r2t, ln2h=ln2h, rln2h=rln2h,
                     rr2t=rr2t: dve(
                lambda e: e.tensor_add(out=r2t[:], in0=xb[:], in1=ln2h[:]),
                reads=[rx, rln2h], writes=[rr2t], tag="resid2"))
            o.append(lambda j=j, r2t=r2t, rr2t=rr2t: dma(
                "scalar", lambda e: e.dma_start(
                    out=r2_scr[j * 128:(j + 1) * 128, :], in_=r2t[:]),
                reads=[rr2t], writes=["r2d"], lane=f"spr{j % 2}", tag="spill_r2"))

            layernorm(r2t, rr2t, x2t, rx2t, LNS_EPS, 2 + 4 * par,
                      f"ln3_{j}", defer=o)
            o.append(lambda j=j, x2t=x2t, rx2t=rx2t: dma(
                "scalar", lambda e: e.dma_start(
                    out=x2_scr[j * 128:(j + 1) * 128, :], in_=x2t[:]),
                reads=[rx2t], writes=["x2d"], lane=f"spx{j % 2}", tag="spill_x2"))
            xfT = x2fmT[par]
            rxfT = f"x2fmT{par}"
            xfTb = x2fmTb[par]
            rxfTb = f"x2fmTb{par}"
            for k in range(ND):
                o.append(lambda k=k, par=par, x2t=x2t, rx2t=rx2t: pe(
                    lambda e: e.transpose(
                        out=ps[par][:, k * 128:(k + 1) * 128],
                        in_=x2t[:, k * 128:(k + 1) * 128], identity=identf[:]),
                    reads=[rx2t, "identf"],
                    writes=[f"ps{par}a" if k < 4 else f"ps{par}b"], tag="x2tr"))
            o.append(lambda xfT=xfT, par=par, rxfT=rxfT: dve(
                lambda e: e.tensor_copy(xfT[:], ps[par][:]),
                reads=[f"ps{par}a", f"ps{par}b"], writes=[rxfT], tag="x2ev"))
            o.append(lambda xfTb=xfTb, xfT=xfT, rxfT=rxfT, rxfTb=rxfTb: act(
                lambda e: e.activation(out=xfTb[:], in_=xfT[:], func=AF.Identity),
                reads=[rxfT], writes=[rxfTb], tag="x2evb"))
            o.append(lambda j=j, xfTb=xfTb, rxfTb=rxfTb: dma(
                "gpsimd", lambda e: e.dma_start(
                    out=x2fm_scr.rearrange("(k p) t -> p k t", p=128)[
                        :, :, j * 128:(j + 1) * 128],
                    in_=xfTb.ap().rearrange("p (k t) -> p k t", k=ND)),
                reads=[rxfTb], writes=["x2fmd"], lane=f"fmsp{j % 2}",
                tag="spill_fm"))
            for k in range(ND):
                o.append(lambda k=k, xfT=xfT, rxfT=rxfT, lps=lps, rlga=rlga: pe(
                    lambda e: e.matmul(
                        out=lps[:, 0:E], lhsT=xfT[:, k * 128:(k + 1) * 128],
                        rhs=wrf[:, k * E:(k + 1) * E],
                        start=(k == 0), stop=(k == ND - 1)),
                    reads=[rxfT, "wrf"], writes=[rlga], tag="lgmm"))
            o.append(lambda lgt=lgt, lps=lps, rlga=rlga, rlgt=rlgt: dve(
                lambda e: e.tensor_copy(lgt[:, 0:E], lps[:, 0:E]),
                reads=[rlga], writes=[rlgt], tag="lgev"))
            if dbg:
                o.append(lambda j=j, lgt=lgt, rlgt=rlgt: dma(
                    "scalar", lambda e: e.dma_start(
                        out=dbg_lg[j * 128:(j + 1) * 128, :], in_=lgt[:, 0:E]),
                    reads=[rlgt], lane=f"dbg{j}", tag="dbg_lg"))
            o.append(lambda lgt=lgt, rlgt=rlgt: dve(
                lambda e: e.reduce_max(out=lgt[:, 4:5], in_=lgt[:, 0:E], axis=AX.X),
                reads=[rlgt], writes=[rlgt], tag="mmax"))
            o.append(lambda lgt=lgt, eqt=eqt, rlgt=rlgt, reqt=reqt: dve(
                lambda e: e.tensor_scalar(out=eqt[:], in0=lgt[:, 0:E],
                                          scalar1=lgt[:, 4:5], scalar2=None,
                                          op0=ALU.is_equal),
                reads=[rlgt], writes=[reqt], tag="meq"))
            mc = mcols.ap()[:, j * E:(j + 1) * E]
            o.append(lambda mc=mc, eqt=eqt, reqt=reqt: dve(
                lambda e: e.tensor_copy(mc[:, 0:1], eqt[:, 0:1]),
                reads=[reqt], writes=["mcols"], tag="m0"))
            o.append(lambda cum=cum, eqt=eqt, reqt=reqt, rcum=rcum: dve(
                lambda e: e.tensor_copy(cum[:, 0:1], eqt[:, 0:1]),
                reads=[reqt], writes=[rcum]))
            for e_ in range(1, E):
                o.append(lambda cum=cum, rcum=rcum: dve(
                    lambda e: e.tensor_scalar(
                        out=cum[:, 1:2], in0=cum[:, 0:1], scalar1=-1.0,
                        scalar2=1.0, op0=ALU.mult, op1=ALU.add),
                    reads=[rcum], writes=[rcum]))
                o.append(lambda e_=e_, mc=mc, eqt=eqt, cum=cum, reqt=reqt,
                         rcum=rcum: dve(
                    lambda e: e.tensor_mul(
                        out=mc[:, e_:e_ + 1], in0=eqt[:, e_:e_ + 1],
                        in1=cum[:, 1:2]),
                    reads=[reqt, rcum], writes=["mcols"]))
                if e_ < E - 1:
                    o.append(lambda e_=e_, mc=mc, cum=cum, rcum=rcum: dve(
                        lambda e: e.tensor_add(
                            out=cum[:, 0:1], in0=cum[:, 0:1],
                            in1=mc[:, e_:e_ + 1]),
                        reads=[rcum, "mcols"], writes=[rcum]))
            if dbg:
                o.append(lambda j=j, mc=mc: dma(
                    "scalar", lambda e: e.dma_start(
                        out=dbg_m[j * 128:(j + 1) * 128, :], in_=mc),
                    reads=["mcols"], lane=f"dbg{j}", tag="dbg_m"))
            o.append(lambda mc=mc, lps=lps, rlgb=rlgb: pe(
                lambda e: e.transpose(
                    out=lps[0:4, 512:640], in_=mc, identity=identf[:]),
                reads=["mcols", "identf"], writes=[rlgb], tag="mtr"))
            m4f, m4b = m4f_[par], m4b_[par]
            rm4f, rm4b = f"m4f{par}", f"m4b{par}"
            o.append(lambda lps=lps, rlgb=rlgb, m4f=m4f, rm4f=rm4f: dve(
                lambda e: e.tensor_copy(m4f[:], lps[0:4, 512:640]),
                reads=[rlgb], writes=[rm4f], tag="m4ev"))
            o.append(lambda m4b=m4b, m4f=m4f, rm4f=rm4f, rm4b=rm4b: dve(
                lambda e: e.tensor_copy(m4b[:], m4f[:]),
                reads=[rm4f], writes=[rm4b]))
            for e_ in range(E):
                o.append(lambda e_=e_, j=j, m4b=m4b, rm4b=rm4b: dma(
                    "scalar", lambda e: e.dma_start(
                        out=m2r[e_][0:1, j * 128:(j + 1) * 128],
                        in_=m4b[e_:e_ + 1, :]),
                    reads=[rm4b], writes=[f"m2r{e_}"], lane=f"mrow{j % 2}",
                    tag="mrow"))
            o.append(lambda j=j, m4b=m4b, rm4b=rm4b: dma(
                "scalar", lambda e: e.dma_start(
                    out=m4r[0:4, j * 128:(j + 1) * 128], in_=m4b[:]),
                reads=[rm4b], writes=["m4r"], lane=f"mrow{j % 2}", tag="mrow4"))
            return o

        if stage >= 4:
            for jp in range(0, 4, 2):
                oa, ob = a4_ops(jp), a4_ops(jp + 1)
                for i in range(max(len(oa), len(ob))):
                    if i < len(oa):
                        oa[i]()
                    if i < len(ob):
                        ob[i]()

    nattn = NQ if stage >= 5 else (1 if stage >= 1.2 else 0)
    if nattn:
        emit_A1(0)
    for q in range(nattn):
        emit_chunks(q)
        if q + 1 < nattn:
            emit_A1(q + 1)
        emit_A4(q)

    S.fence()

    x2fq_r = x2fq.ap().rearrange("p (k t) -> p k t", k=ND)

    def emit_F0(q):
        dma("sync", lambda e, q=q: e.dma_start(
            out=x2fq_r,
            in_=x2fm_scr.rearrange("(k p) t -> p k t", p=128)[
                :, :, q * QT:(q + 1) * QT]),
            reads=["x2fmd"],
            writes=["x2fq"], lane="x2fq", tag="x2fq")
        for e_ in range(E):
            bank = f"ps3{'ab'[e_ % 2]}"
            off = (e_ % 2) * 512
            pe(lambda e, e_=e_, q=q, off=off: e.matmul(
                out=ps[3][:, off:off + QT], lhsT=onesb[0:1, :],
                rhs=m2r[e_][0:1, q * QT:(q + 1) * QT], start=True, stop=True),
               reads=[f"m2r{e_}", "onesb"], writes=[bank], tag="mbc")
            dve(lambda e, e_=e_, off=off: e.tensor_copy(
                maskb[e_][:], ps[3][:, off:off + QT]),
                reads=[bank], writes=[f"maskb{e_}"], tag="mbev")
        for e_ in range(E):
            eng_fn = dve
            for k in range(ND):
                eng_fn(lambda e, e_=e_, k=k: e.tensor_mul(
                    out=x2fe[e_][:, k * QT:(k + 1) * QT],
                    in0=x2fq[:, k * QT:(k + 1) * QT], in1=maskb[e_][:]),
                    reads=["x2fq", f"maskb{e_}"], writes=[f"x2fe{e_}"], tag="xfe")

    def emit_Z(q):
        for ig in range(NI // 2):
            wb = w1bb[ig % 2]
            rwb = f"w1bb{ig % 2}"
            for part in range(8):
                c0 = part * 1024
                dma("sync", lambda e, ig=ig, wb=wb, c0=c0: e.dma_start(
                    out=wb[:, c0:c0 + 1024], in_=w1b_scr[ig][:, c0:c0 + 1024]),
                    reads=["w1scr"], writes=[f"{rwb}p{part}"],
                    lane=f"{rwb}p{part}", tag="w1ld")
            for ii in range(2):
                it = ig * 2 + ii
                zbank = f"ps{(it % 8) // 2}{'ab'[it % 2]}"
                zoff = (it % 2) * 512
                zp = ps[(it % 8) // 2]
                for e_ in range(E):
                    for k in range(ND):
                        pe(lambda e, e_=e_, k=k, ii=ii, wb=wb, zp=zp, zoff=zoff:
                           e.matmul(
                            out=zp[:, zoff:zoff + QT],
                            lhsT=wb[:, ((e_ * ND + k) * 256) + ii * 128:
                                    ((e_ * ND + k) * 256) + ii * 128 + 128],
                            rhs=x2fe[e_][:, k * QT:(k + 1) * QT],
                            start=(e_ == 0 and k == 0), stop=False),
                           reads=[f"{rwb}p{(e_ * ND + k) // 4}",
                                  f"x2fe{e_}"], writes=[zbank],
                           tag="zmm")
                pe(lambda e, it=it, q=q, zp=zp, zoff=zoff: e.matmul(
                    out=zp[:, zoff:zoff + QT],
                    lhsT=b1b4[0:4, it * 128:(it + 1) * 128],
                    rhs=m4r[0:4, q * QT:(q + 1) * QT],
                    start=False, stop=True),
                   reads=["b1b4", "m4r"], writes=[zbank], tag="zbias")
                act(lambda e, it=it, zp=zp, zoff=zoff: e.activation(
                    out=interm[:, it * QT:(it + 1) * QT],
                    in_=zp[:, zoff:zoff + QT], func=AF.Gelu),
                    reads=[zbank], writes=[f"interm{it}"], tag="gelu")

    nffn = 0 if stage < 6 else (NQ if stage >= 7 else 1)
    if nffn:
        emit_F0(0)
    for q in range(nffn):
        emit_Z(q)
        if q + 1 < nffn:
            emit_F0(q + 1)
        for ic in range(NI if stage >= 7 else 0):
            wo_b = wobb[ic % 2]
            rwo = f"wobb{ic % 2}"
            dma("sync", lambda e, ic=ic, wo_b=wo_b: e.dma_start(
                out=wo_b[:], in_=wob_scr[ic * 128:(ic + 1) * 128, :]),
                reads=["woscr"], writes=[rwo], lane=rwo, tag="wold")
            for tt in range(4):
                for half in range(2):
                    pe(lambda e, ic=ic, tt=tt, half=half, wo_b=wo_b: e.matmul(
                        out=ps[tt][:, half * 512:(half + 1) * 512],
                        lhsT=interm[:, ic * QT + tt * 128: ic * QT + (tt + 1) * 128],
                        rhs=wo_b[:, half * 512:(half + 1) * 512],
                        start=(ic == 0), stop=(ic == NI - 1)),
                       reads=[rwo, f"interm{ic}"],
                       writes=[f"ps{tt}{'ab'[half]}"], tag="ymm")

        for tt in range(4 if stage >= 7 else 0):
            j = q * 4 + tt
            x2r = x2rel[tt % 2]
            rx2r = f"x2rel{tt % 2}"
            r2r = r2rel[tt % 2]
            rr2r = f"r2rel{tt % 2}"
            dma("scalar", lambda e, j=j, x2r=x2r: e.dma_start(
                out=x2r[:], in_=x2_scr[j * 128:(j + 1) * 128, :]),
                reads=["x2d"], writes=[rx2r], lane=rx2r, tag="x2rel")
            dma("scalar", lambda e, j=j, r2r=r2r: e.dma_start(
                out=r2r[:], in_=r2_scr[j * 128:(j + 1) * 128, :]),
                reads=["r2d"], writes=[rr2r], lane=rr2r, tag="r2rel")
            dve(lambda e, tt=tt, x2r=x2r: e.tensor_add(
                out=ubuf[:], in0=ps[tt][:], in1=x2r[:]),
                reads=[f"ps{tt}a", f"ps{tt}b", rx2r], writes=["ubuf"], tag="yadd")
            dve(lambda e: e.tensor_add(out=ubuf[:], in0=ubuf[:], in1=bo_b[:]),
                reads=["ubuf", "o_b"], writes=["ubuf"], tag="boadd")
            layernorm(ubuf, "ubuf", y2buf, "y2buf", LN_EPS, 3, f"lni_{j}")
            pool(lambda e: e.tensor_mul(out=y2buf[:], in0=y2buf[:], in1=gam_b[:]),
                 reads=["y2buf", "g_b"], writes=["y2buf"], tag="gmul")
            pool(lambda e: e.tensor_add(out=y2buf[:], in0=y2buf[:], in1=bet_b[:]),
                 reads=["y2buf", "b_b"], writes=["y2buf"], tag="badd")
            if dbg:
                dma("scalar", lambda e, j=j: e.dma_start(
                    out=dbg_u2[j * 128:(j + 1) * 128, :], in_=y2buf[:]),
                    reads=["y2buf"], lane=f"dbg{j}", tag="dbg_u2")
            layernorm(y2buf, "y2buf", ubuf, "ubuf", LNS_EPS, 0, f"lns_{j}")
            dve(lambda e, r2r=r2r: e.tensor_add(out=ubuf[:], in0=ubuf[:], in1=r2r[:]),
                reads=["ubuf", rr2r], writes=["ubuf"], tag="fadd")
            dma("scalar", lambda e, j=j: e.dma_start(
                out=out_h[j * 128:(j + 1) * 128, :], in_=ubuf[:]),
                reads=["ubuf"], writes=[f"out{j}"], lane=f"out{j % 2}", tag="outdma")

    if stage >= 7:
        S.add("sync", lambda e: e.nop(),
              deps=[S.lw[f"out{j}"] for j in range(NT)], tag="final_wait")
    else:
        S.fence(tag="trunc_fence")
        od = dma("scalar", lambda e: e.dma_start(
            out=out_h[0:128, 0:128], in_=identf[:]), reads=["identf"],
            lane="out0", tag="trunc_out")
        S.add("sync", lambda e: e.nop(), deps=[od], tag="final_wait")

    A_per.allocate()
    A_attn.allocate()
    A_attn.free()
    A_ffn.allocate()
    S.finalize(nc, es)
    with nc.Block() as block:
        @block.sync
        def _(eng):
            S.replay("sync", eng)

        @block.gpsimd
        def _(eng):
            S.replay("gpsimd", eng)

        @block.vector
        def _(eng):
            S.replay("vector", eng)

        @block.scalar
        def _(eng):
            S.replay("scalar", eng)

        @block.tensor
        def _(eng):
            S.replay("tensor", eng)

    es.close()
    return nc


_NC_CACHE = {}


def kernel(L0_attn_x, L0_attn_a_param, L0_attn_w_in, L0_attn_b_in, L0_attn_w_a,
           L0_attn_b_a, L0_ffn_W_router, L0_ffn_W1, L0_ffn_b1, L0_ffn_Wo,
           L0_ffn_bo, L0_ffn_ln_gamma, L0_ffn_ln_beta, _dbg=False):
    if _dbg not in _NC_CACHE:
        _NC_CACHE[_dbg] = build(dbg=_dbg)
    nc = _NC_CACHE[_dbg]
    c = np.ascontiguousarray
    shared = dict(
        a_param=c(L0_attn_a_param, dtype=np.float32),
        w_in=c(L0_attn_w_in, dtype=np.float32),
        b_in=c(L0_attn_b_in, dtype=np.float32),
        w_a=c(L0_attn_w_a, dtype=np.float32),
        b_a=c(L0_attn_b_a, dtype=np.float32),
        w_router=c(L0_ffn_W_router, dtype=np.float32),
        w1=c(L0_ffn_W1, dtype=np.float32),
        b1=c(L0_ffn_b1, dtype=np.float32),
        wo=c(L0_ffn_Wo, dtype=np.float32),
        bo=c(L0_ffn_bo, dtype=np.float32),
        ln_gamma=c(L0_ffn_ln_gamma, dtype=np.float32),
        ln_beta=c(L0_ffn_ln_beta, dtype=np.float32),
    )
    in_maps = [dict(shared, x=c(L0_attn_x[b], dtype=np.float32)) for b in range(8)]
    res = run_bass_kernel_spmd(nc, in_maps, core_ids=list(range(8)))
    out = np.stack([np.asarray(res.results[b]["out"], dtype=np.float32)
                    for b in range(8)], axis=0)
    if _dbg:
        return out, res
    return out


# version 24 (final)
# speedup vs baseline: 1.0511x; 1.0511x over previous
import contextlib
import numpy as np
import concourse.bass as bass
import concourse.mybir as mybir
from concourse.bass_utils import run_bass_kernel_spmd

F32 = mybir.dt.float32
BF16 = mybir.dt.bfloat16
AF = mybir.ActivationFunctionType
ALU = mybir.AluOpType
AX = mybir.AxisListType
f32 = np.float32

T, D, H, BW, I, E = 2048, 1024, 8, 128, 4096, 4
NT, ND, NQ, QT = 16, 8, 4, 512
NI = I // 128
C1 = float(np.log1p(np.float64(-1e-5)))
A_MAX = float(np.float32(1.0 - 1e-5))
C7U = float(np.log(1e7) * (1.0 + 1e-6))
MIN_DECAY, EPS, SCAN_EPS, LN_EPS, LNS_EPS = 1e-3, 1e-5, 1e-7, 1e-12, 1e-5

ENGS = ("sync", "gpsimd", "vector", "scalar", "tensor")


class _Op:
    __slots__ = ("eng", "emit", "deps", "dma", "lane", "idx", "ev", "waits", "tag", "_key")


class Sched:

    def __init__(self, elide_same_engine=()):
        self.elide = set(elide_same_engine)
        self.ops = []
        self.lw = {}
        self.rd = {}
        self.global_dep = None

    def add(self, eng, emit, reads=(), writes=(), dma=False, deps=(), tag="",
            lane=None):
        op = _Op()
        op.eng, op.emit, op.dma, op.tag = eng, emit, dma, tag
        op.lane = lane
        op.idx = len(self.ops)
        d = set()
        for x in deps:
            if x is not None:
                d.add(x)
        if self.global_dep is not None:
            d.add(self.global_dep)
        for r in reads:
            w = self.lw.get(r)
            if w is not None:
                d.add(w)
        for w_ in writes:
            w = self.lw.get(w_)
            if w is not None:
                d.add(w)
            for r in self.rd.get(w_, ()):
                d.add(r)
        d.discard(op)
        op.deps = d
        for r in reads:
            self.rd.setdefault(r, []).append(op)
        for w_ in writes:
            self.lw[w_] = op
            self.rd[w_] = []
        self.ops.append(op)
        return op

    def fence(self, eng="vector", tag="fence"):
        deps = set(self.lw.values())
        for lst in self.rd.values():
            deps.update(lst)
        op = self.add(eng, lambda e: e.engine_nop(), deps=deps, tag=tag)
        self.global_dep = op
        self.lw = {}
        self.rd = {}
        return op

    def finalize(self, nc, es):
        if self.elide:
            for op in self.ops:
                op.deps = {
                    d for d in op.deps
                    if not (d.eng == op.eng and op.eng in self.elide
                            and not d.dma and not op.dma)
                }
        keys = []
        for op in self.ops:
            op.ev = None
            k = ("D", op.eng, op.lane) if op.dma else ("C", op.eng)
            if k not in keys:
                keys.append(k)
            op._key = k
        def fam(op):
            return (op.eng, op.lane.rstrip("0123456789")) if op.dma else None
        for op in self.ops:
            if op.dma:
                op.deps = {d for d in op.deps
                           if not (d.dma and fam(d) == fam(op))}
        for op in self.ops:
            if len(op.deps) > 1:
                best = {}
                keep = set()
                for d in op.deps:
                    if d.dma:
                        keep.add(d)
                    else:
                        cur = best.get(d.eng)
                        if cur is None or d.idx > cur.idx:
                            best[d.eng] = d
                op.deps = keep | set(best.values())
        self.sems = {}
        for i, k in enumerate(keys):
            nm = "_".join(str(p) for p in k)
            self.sems[k] = es.enter_context(nc.semaphore(f"s{i}_{nm}"))
        needed = set()
        for op in self.ops:
            needed.update(op.deps)
        ctr = {k: 0 for k in keys}
        for op in self.ops:
            if op in needed or op.dma:
                k = op._key
                ctr[k] += 16 if op.dma else 1
                op.ev = (k, ctr[k])
        op_vc = {}
        engine_vc = {e: {} for e in ENGS}
        for op in self.ops:
            vc = {}
            for d in op.deps:
                for k, v in op_vc[d].items():
                    if vc.get(k, 0) < v:
                        vc[k] = v
            evc = engine_vc[op.eng]
            wmax = {}
            for d in op.deps:
                k, v = d.ev
                if evc.get(k, 0) >= v:
                    continue
                if wmax.get(k, 0) < v:
                    wmax[k] = v
                for k2, v2 in op_vc[d].items():
                    if evc.get(k2, 0) < v2:
                        evc[k2] = v2
            op.waits = sorted(wmax.items(), key=lambda kv: str(kv[0]))
            if op.ev is not None:
                k, v = op.ev
                if vc.get(k, 0) < v:
                    vc[k] = v
            for k, v in evc.items():
                if vc.get(k, 0) < v:
                    vc[k] = v
            op_vc[op] = vc
        fin = self.ops[-1]
        assert fin.tag == "final_wait"
        evc = engine_vc[fin.eng]
        have = dict(fin.waits)
        for k in keys:
            if k[0] == "D" and ctr[k] > max(evc.get(k, 0), have.get(k, 0)):
                have[k] = ctr[k]
        fin.waits = sorted(have.items(), key=lambda kv: str(kv[0]))

    def replay(self, engine, eng):
        for op in self.ops:
            if op.eng != engine:
                continue
            for k, v in op.waits:
                eng.wait_ge(self.sems[k], v)
            inst = op.emit(eng)
            if op.ev is not None:
                inst.then_inc(self.sems[op.ev[0]], 16 if op.dma else 1)


def build(dbg=False, stage=99, elide=("tensor",)):
    nc = bass.Bass("TRN2", target_bir_lowering=False)
    es = contextlib.ExitStack()
    S = Sched(elide_same_engine=elide or ())

    dr = lambda name, shape, dt, kind: nc.dram_tensor(name, shape, dt, kind=kind).ap()
    x_h = dr("x", [T, D], F32, "ExternalInput")
    ap_h = dr("a_param", [D], F32, "ExternalInput")
    win_h = dr("w_in", [H, BW, BW], F32, "ExternalInput")
    bin_h = dr("b_in", [H, BW], F32, "ExternalInput")
    wa_h = dr("w_a", [H, BW, BW], F32, "ExternalInput")
    ba_h = dr("b_a", [H, BW], F32, "ExternalInput")
    wr_h = dr("w_router", [D, E], F32, "ExternalInput")
    w1_h = dr("w1", [E, D, I], F32, "ExternalInput")
    b1_h = dr("b1", [E, I], F32, "ExternalInput")
    wo_h = dr("wo", [I, D], F32, "ExternalInput")
    bo_h = dr("bo", [D], F32, "ExternalInput")
    lng_h = dr("ln_gamma", [D], F32, "ExternalInput")
    lnb_h = dr("ln_beta", [D], F32, "ExternalInput")
    out_h = dr("out", [T, D], F32, "ExternalOutput")

    scr_kind = "ExternalOutput" if dbg else "Internal"
    r2_scr = dr("r2_scr", [T, D], F32, scr_kind)
    x2_scr = dr("x2_scr", [T, D], F32, scr_kind)
    x2fm_scr = dr("x2fm_scr", [D, T], BF16, "Internal")
    w1b_scr = dr("w1b_scr", [E, D, I], BF16, "Internal")
    wob_scr = dr("wob_scr", [I, D], BF16, "Internal")
    vec_scr = dr("vec_scr", [D], F32, "Internal")
    if dbg:
        dbg_h = dr("d_h", [D, T], F32, "ExternalOutput")
        dbg_lg = dr("d_lg", [T, E], F32, "ExternalOutput")
        dbg_m = dr("d_m", [T, E], F32, "ExternalOutput")
        dbg_u2 = dr("d_u2", [T, D], F32, "ExternalOutput")
        dbg_x1 = dr("d_x1", [T, D], F32, "ExternalOutput")

    A_per = nc.auto_arena()
    A_attn = nc.auto_arena()
    A_ffn = nc.auto_arena()

    class _W:
        def __init__(self, arena):
            self.arena = arena
        def t(self, name, shape, dtype):
            return self.arena.sbuf_tensor(name, shape, dtype)
    A = _W(A_per)
    identf = A.t("identf", [128, 128], F32)
    iot = A.t("iot", [128, 128], F32)
    onesb = A.t("onesb", [1, 128], BF16)
    onesf = A.t("onesf", [1, 128], F32)
    w_inb = A.t("w_inb", [128, H * 128], F32)
    w_ab = A.t("w_ab", [128, H * 128], F32)
    hbin = A.t("hbin", [128, H], F32)
    hba = A.t("hba", [128, H], F32)
    nhc = A.t("nhc", [128, ND], F32)
    wrf = A.t("wrf", [128, ND * E], F32)
    m2r = [A.t(f"m2r{e}", [1, T], BF16) for e in range(E)]
    m4r = A.t("m4r", [4, T], BF16)
    b1b4 = A.t("b1b4", [4, I], BF16)
    mcols = A.t("mcols", [128, NT * E], F32)
    gam_b = A.t("gam_b", [128, D], F32)
    bet_b = A.t("bet_b", [128, D], F32)
    bo_b = A.t("bo_b", [128, D], F32)
    carryA = A.t("carryA", [128, ND], F32)
    carryB = A.t("carryB", [128, ND], F32)
    scol = [A.t(f"scol{i}", [128, 16], F32) for i in range(8)]
    m4b_ = [A.t(f"m4b{i}", [4, 128], BF16) for i in range(2)]
    m4f_ = [A.t(f"m4f{i}", [4, 128], F32) for i in range(2)]
    lgt_ = [A.t(f"lgt{i}", [128, 8], F32) for i in range(2)]
    eqt_ = [A.t(f"eqt{i}", [128, 4], F32) for i in range(2)]
    cum_ = [A.t(f"cum{i}", [128, 2], F32) for i in range(2)]

    A = _W(A_attn)
    x_t = [A.t(f"x{i}", [128, D], F32) for i in range(4)]
    xr_t = [A.t(f"xr{i}", [128, D], F32) for i in range(2)]
    x1tok = [A.t(f"x1tok{i}", [128, D], F32) for i in range(2)]
    x1fm_ = [A.t(f"x1fm{i}", [128, ND * QT], F32) for i in range(2)]
    trp = {k: [A.t(f"tr_{k}{i}", [128, QT], F32) for i in range(3)]
           for k in ("tga", "tgxr", "asqS", "nrm", "ce", "p")}
    h_fm = A.t("h_fm", [128, ND * QT], F32)
    htok_ = [A.t(f"htok{i}", [128, D], F32) for i in range(2)]
    r2t_ = [A.t(f"r2t{i}", [128, D], F32) for i in range(2)]
    x2t_ = [A.t(f"x2t{i}", [128, D], F32) for i in range(2)]
    ln2h_ = x2t_
    x2fmT = [A.t(f"x2fmT{i}", [128, ND * 128], F32) for i in range(2)]
    x2fmTb = [A.t(f"x2fmTb{i}", [128, ND * 128], BF16) for i in range(2)]
    rowbuf = x2fmT
    braw = A.t("braw", [8, 128], F32)

    A = _W(A_ffn)
    x2fq = A.t("x2fq", [128, ND * QT], BF16)
    maskb = [A.t(f"maskb{e}", [128, QT], BF16) for e in range(E)]
    x2fe = [A.t(f"x2fe{e}", [128, ND * QT], BF16) for e in range(E)]
    interm = A.t("interm", [128, NI * QT], BF16)
    w1bb = [A.t(f"w1bb{i}", [128, E * ND * 256], BF16) for i in range(2)]
    wobb = [A.t(f"wobb{i}", [128, D], BF16) for i in range(2)]
    x2rel = [A.t(f"x2rel{i}", [128, D], F32) for i in range(2)]
    r2rel = [A.t(f"r2rel{i}", [128, D], F32) for i in range(2)]
    ubuf = A.t("ubuf", [128, D], F32)
    y2buf = A.t("y2buf", [128, D], F32)

    ps = [es.enter_context(nc.psum_tensor(f"ps{i}", [128, 1024], F32))
          for i in range(4)]

    def dve(emit, reads=(), writes=(), tag="", lane=None):
        return S.add("vector", emit, reads, writes, tag=tag)

    def act(emit, reads=(), writes=(), tag="", lane=None):
        return S.add("scalar", emit, reads, writes, tag=tag)

    def pool(emit, reads=(), writes=(), tag="", lane=None):
        return S.add("gpsimd", emit, reads, writes, tag=tag)

    def pe(emit, reads=(), writes=(), tag="", lane=None):
        return S.add("tensor", emit, reads, writes, tag=tag)

    def dma(q, emit, reads=(), writes=(), lane="misc", tag=""):
        return S.add(q, emit, reads, writes, dma=True, tag=tag, lane=lane)

    for j in range(4):
        dma("sync", lambda e, j=j: e.dma_start(
            out=x_t[j][:], in_=x_h[j * 128:(j + 1) * 128, :]),
            writes=[f"x{j}"], lane=f"x{j}", tag="xload")
    for e_ in range(E):
        dma("gpsimd", lambda e, e_=e_: e.dma_start(
            out=b1b4[e_:e_ + 1, :], in_=b1_h[e_][None, :]),
            writes=["b1b4"], lane="b1b", tag="b1ld")

    pool(lambda e: e.iota(iot[:], pattern=[[1, 128]], base=0, channel_multiplier=-1,
                          allow_small_or_imprecise_dtypes=True),
         writes=["iot"], tag="iota")
    dve(lambda e: e.tensor_single_scalar(out=identf[:], in_=iot[:], scalar=0.0,
                                         op=ALU.is_equal),
        reads=["iot"], writes=["identf"])
    dve(lambda e: e.memset(onesf[:], 1.0), writes=["onesf"])
    dve(lambda e: e.tensor_copy(onesb[:], onesf[:]), reads=["onesf"], writes=["onesb"])

    for h in range(H):
        dma("gpsimd", lambda e, h=h: e.dma_start(
            out=w_inb[:, h * 128:(h + 1) * 128], in_=win_h[h]),
            writes=["w_inb"], lane="pregp", tag="w_inb")
        dma("gpsimd", lambda e, h=h: e.dma_start(
            out=w_ab[:, h * 128:(h + 1) * 128], in_=wa_h[h]),
            writes=["w_ab"], lane="pregp", tag="w_ab")


    for name, hsrc, dstc, rdst in (("bi", bin_h, hbin, "hbin"),
                                   ("ba", ba_h, hba, "hba")):
        dma("sync", lambda e, hsrc=hsrc: e.dma_start(out=braw[0:8, 0:128], in_=hsrc),
            reads=["braw"], writes=["braw"], lane="braw", tag="braw")
        pe(lambda e: e.transpose(out=ps[3][0:128, 0:8], in_=braw[0:8, 0:128],
                                 identity=identf[0:8, 0:8]),
           reads=["braw", "identf"], writes=["ps3a"], tag="btr")
        dve(lambda e, dstc=dstc: e.tensor_scalar_mul(
            out=dstc[:], in0=ps[3][0:128, 0:8], scalar1=0.5),
            reads=["ps3a"], writes=[rdst], tag="bev")

    r0, r1 = rowbuf[0], rowbuf[1]
    dma("sync", lambda e: e.dma_start(out=r0[0:1, 0:D], in_=ap_h[None, :]),
        writes=["r0"], lane="apr")
    ev_abs = act(lambda e: e.activation(out=r1[0:1, 0:D], in_=r0[0:1, 0:D], func=AF.Abs),
                 reads=["r0"], writes=["r1"])
    act(lambda e: e.activation(out=r1[0:1, 0:D], in_=r1[0:1, 0:D], func=AF.Exp,
                               scale=f32(-1.0)), reads=["r1"], writes=["r1"])
    act(lambda e: e.activation(out=r1[0:1, 0:D], in_=r1[0:1, 0:D], func=AF.Ln,
                               bias=f32(1.0)), reads=["r1"], writes=["r1"])
    dve(lambda e: e.tensor_scalar_max(out=r0[0:1, 0:D], in0=r0[0:1, 0:D], scalar1=0.0),
        reads=["r0"], writes=["r0"], lane="apr")
    dve(lambda e: e.tensor_add(out=r0[0:1, 0:D], in0=r0[0:1, 0:D], in1=r1[0:1, 0:D]),
        reads=["r0", "r1"], writes=["r0"], lane="apr")
    dve(lambda e: e.tensor_scalar(out=r0[0:1, 0:D], in0=r0[0:1, 0:D],
                                  scalar1=MIN_DECAY, scalar2=-0.5,
                                  op0=ALU.add, op1=ALU.mult),
        reads=["r0"], writes=["r0"], lane="apr")
    dma("sync", lambda e: e.dma_start(out=vec_scr[None, :], in_=r0[0:1, 0:D]),
        reads=["r0"], writes=["vec_scr"], lane="vscs")
    dma("sync", lambda e: e.dma_start(
        out=braw[0:8, 0:128], in_=vec_scr.rearrange("(c p) -> c p", p=128)),
        reads=["vec_scr", "braw"], writes=["braw"], lane="vscl")
    pe(lambda e: e.transpose(out=ps[3][0:128, 0:8], in_=braw[0:8, 0:128],
                             identity=identf[0:8, 0:8]),
       reads=["braw", "identf"], writes=["ps3a"], tag="nhctr")
    dve(lambda e: e.tensor_copy(nhc[:], ps[3][0:128, 0:8]),
        reads=["ps3a"], writes=["nhc"], tag="nhcev")

    with nc.allow_non_contiguous_dma(reason="tiny router weight load"):
        wr_r = wr_h.rearrange("(k p) e -> p k e", p=128)
        wrf_r = wrf.ap().rearrange("p (k e) -> p k e", k=ND)
        dma("scalar", lambda e: e.dma_start(out=wrf_r, in_=wr_r,
                                            allow_slow_non_contiguous=True),
            writes=["wrf"], lane="wrf")

    for name, src, dst in (("g", lng_h, gam_b), ("b", lnb_h, bet_b), ("o", bo_h, bo_b)):
        dma("sync", lambda e, src=src: e.dma_start(out=r1[0:1, 0:D], in_=src[None, :]),
            reads=["r1"], writes=["r1"], lane="brow", tag="bcast_row")
        for half in range(2):
            pe(lambda e, half=half: e.matmul(
                out=ps[3][:, half * 512:(half + 1) * 512], lhsT=onesf[0:1, :],
                rhs=r1[0:1, half * 512:(half + 1) * 512], start=True, stop=True),
               reads=["r1", "onesf"], writes=[f"ps3{'ab'[half]}"], tag="bcast_mm")
        dve(lambda e, dst=dst: e.tensor_copy(dst[:], ps[3][:]),
            reads=["ps3a", "ps3b"], writes=[name + "_b"])

    dve(lambda e: e.memset(carryA[:], 0.0), writes=["carryA"])
    dve(lambda e: e.memset(carryB[:], 0.0), writes=["carryB"])

    _pre_res = ["identf", "identb", "onesb", "onesf", "w_inb", "w_ab", "hbin",
                "hba", "nhc", "g_b", "b_b", "o_b", "carryA", "carryB"]
    dve(lambda e: e.engine_nop(), reads=list(_pre_res), writes=list(_pre_res),
        tag="prelude_barrier")

    for e_ in range(E):
        for k in range(ND):
            dma("gpsimd", lambda e, e_=e_, k=k: e.dma_start(
                out=w1b_scr[e_, k * 128:(k + 1) * 128, :],
                in_=w1_h[e_, k * 128:(k + 1) * 128, :]),
                writes=["w1scr"], lane="cast", tag="w1cast")
    for k in range(NI):
        dma("gpsimd", lambda e, k=k: e.dma_start(
            out=wob_scr[k * 128:(k + 1) * 128, :],
            in_=wo_h[k * 128:(k + 1) * 128, :]),
            writes=["woscr"], lane="cast", tag="wocast")


    def layernorm(src_t, rs, dst_t, rd, eps, ci, tag, defer=None):
        sc = scol[ci]
        rc = f"scol{ci}"
        o = []
        o.append(lambda: dve(lambda e: e.bn_stats(out=sc[:, 0:6],
                                                  in_=src_t[:, 0:512]),
                             reads=[rs], writes=[rc], tag=tag + "_bst"))
        o.append(lambda: dve(lambda e: e.bn_stats(out=sc[:, 6:12],
                                                  in_=src_t[:, 512:1024]),
                             reads=[rs], writes=[rc], tag=tag + "_bs2"))
        o.append(lambda: dve(lambda e: e.bn_aggr(out=sc[:, 12:14], in_=sc[:, 0:12]),
                             reads=[rc], writes=[rc], tag=tag + "_bag"))
        o.append(lambda: act(lambda e: e.activation(
            out=sc[:, 14:15], in_=sc[:, 13:14], func=AF.Ln, bias=f32(eps)),
            reads=[rc], writes=[rc], tag=tag + "_ln"))
        o.append(lambda: act(lambda e: e.activation(
            out=sc[:, 14:15], in_=sc[:, 14:15], func=AF.Exp, scale=f32(-0.5)),
            reads=[rc], writes=[rc], tag=tag + "_ex"))
        o.append(lambda: dve(lambda e: e.scalar_tensor_tensor(
            out=sc[:, 15:16], in0=sc[:, 12:13], scalar=-1.0, in1=sc[:, 14:15],
            op0=ALU.mult, op1=ALU.mult),
            reads=[rc], writes=[rc]))
        o.append(lambda: act(lambda e: e.activation(
            out=dst_t[:], in_=src_t[:], func=AF.Identity,
            scale=sc[:, 14:15], bias=sc[:, 15:16]),
            reads=[rs, rc], writes=[rd], tag=tag + "_ap"))
        if defer is None:
            for f in o:
                f()
        else:
            defer.extend(o)


    def emit_A1(q):
        if q >= 1:
            for jj in range(4):
                j = q * 4 + jj
                dma("sync", lambda e, j=j: e.dma_start(
                    out=x_t[j % 4][:], in_=x_h[j * 128:(j + 1) * 128, :]),
                    writes=[f"x{j % 4}"], lane=f"x{j % 4}", tag="xload")
        def a1_ops(jj):
            j = q * 4 + jj
            par = j % 2
            xb = x_t[j % 4]
            rx = f"x{j % 4}"
            x1 = x1tok[par]
            rx1 = f"x1tok{par}"
            o = []
            layernorm(xb, rx, x1, rx1, LNS_EPS, 0 + 4 * par, f"ln1_{j}", defer=o)
            if stage < 1.6:
                return o
            if dbg:
                o.append(lambda j=j, x1=x1, rx1=rx1: dma(
                    "scalar", lambda e: e.dma_start(
                        out=dbg_x1[j * 128:(j + 1) * 128, :], in_=x1[:]),
                    reads=[rx1], lane=f"dbg{j}", tag="dbg_x1"))
            for k in range(ND):
                o.append(lambda k=k, x1=x1, rx1=rx1, par=par: pe(
                    lambda e: e.transpose(
                        out=ps[2 + par][:, k * 128:(k + 1) * 128],
                        in_=x1[:, k * 128:(k + 1) * 128], identity=identf[:]),
                    reads=[rx1, "identf"],
                    writes=[f"ps{2 + par}a" if k < 4 else f"ps{2 + par}b"],
                    tag="x1tr"))
            if stage < 1.9:
                return o
            x1fm = x1fm_[q % 2]
            rx1fm = f"x1fm{q % 2}"
            dst = x1fm.ap().rearrange("p (k t) -> p k t", k=ND)[
                :, :, jj * 128:(jj + 1) * 128]
            srcp = ps[2 + par].ap().rearrange("p (k t) -> p k t", k=ND)
            o.append(lambda dst=dst, srcp=srcp, par=par, rx1fm=rx1fm: dve(
                lambda e: e.tensor_copy(dst, srcp),
                reads=[f"ps{2 + par}a", f"ps{2 + par}b"], writes=[rx1fm],
                tag="x1ev"))
            return o

        for jp in range(0, 4, 2):
            oa, ob = a1_ops(jp), a1_ops(jp + 1)
            for i in range(max(len(oa), len(ob))):
                if i < len(oa):
                    oa[i]()
                if i < len(ob):
                    ob[i]()

        return

    def emit_chunks(q):
        def chunk_ops(k):
            x1fm = x1fm_[q % 2]
            rx1fm = f"x1fm{q % 2}"
            par = k % 3
            sfx = str(par)
            tga = trp["tga"][par]
            tgx = trp["tgxr"][par]
            r_ = tgx
            asq = trp["asqS"][par]
            lnm = asq
            S_ = asq
            nrm, ce, p_ = trp["nrm"][par], trp["ce"][par], trp["p"][par]
            rtga = "tga" + sfx
            rtgx = rr = "tgxr" + sfx
            rasq = rlnm = rS = "asqS" + sfx
            rnrm, rce, rp = "nrm" + sfx, "ce" + sfx, "p" + sfx
            rcA, rcB = f"carryA{k}", f"carryB{k}"
            gb = ps[[2, 3, 0][par]]
            gbn = [2, 3, 0][par]
            o = []
            for gi, (wb, rw, hb, rh, tt, rtt) in enumerate(
                    ((w_inb, "w_inb", hbin, "hbin", tgx, rtgx),
                     (w_ab, "w_ab", hba, "hba", tga, rtga))):
                bank = f"ps{gbn}{'ab'[gi]}"
                off = gi * 512
                o.append(lambda k=k, wb=wb, off=off, gb=gb, rw=rw, bank=bank: pe(
                    lambda e: e.matmul(
                        out=gb[:, off:off + 512], lhsT=wb[:, k * 128:(k + 1) * 128],
                        rhs=x1fm[:, k * QT:(k + 1) * QT], start=True, stop=True),
                    reads=[rw, rx1fm], writes=[bank], tag="gmm"))
                o.append(lambda k=k, hb=hb, off=off, gb=gb, tt=tt, rh=rh,
                         bank=bank, rtt=rtt: act(
                    lambda e: e.activation(
                        out=tt[:], in_=gb[:, off:off + 512], func=AF.Tanh,
                        scale=f32(0.5), bias=hb[:, k:k + 1]),
                    reads=[bank, rh], writes=[rtt], tag="tanh"))
            o.append(lambda k=k: dve(lambda e: e.tensor_scalar(
                out=tga[:], in0=tga[:], scalar1=nhc[:, k:k + 1],
                scalar2=nhc[:, k:k + 1], op0=ALU.mult, op1=ALU.add),
                reads=[rtga, "nhc"], writes=[rtga], tag="la"))
            o.append(lambda: dve(lambda e: e.tensor_scalar_min(
                out=tga[:], in0=tga[:], scalar1=C1),
                reads=[rtga], writes=[rtga]))
            o.append(lambda: act(lambda e: e.activation(
                out=asq[:], in_=tga[:], func=AF.Exp, scale=f32(2.0)),
                reads=[rtga], writes=[rasq], tag="expa"))
            o.append(lambda: dve(lambda e: e.tensor_scalar_min(
                out=asq[:], in0=asq[:], scalar1=A_MAX * A_MAX),
                reads=[rasq], writes=[rasq]))
            o.append(lambda: dve(lambda e: e.tensor_scalar(
                out=asq[:], in0=asq[:], scalar1=-1.0, scalar2=1.0,
                op0=ALU.mult, op1=ALU.add),
                reads=[rasq], writes=[rasq]))
            o.append(lambda: act(lambda e: e.activation(
                out=lnm[:], in_=asq[:], func=AF.Ln),
                reads=[rasq], writes=[rlnm], tag="lnmul"))
            o.append(lambda: act(lambda e: e.activation(
                out=lnm[:], in_=lnm[:], func=AF.Exp, scale=f32(0.5),
                bias=f32(float(np.log(0.5)))),
                reads=[rlnm], writes=[rlnm], tag="expmul"))
            o.append(lambda k=k: pool(lambda e: e.tensor_mul(
                out=nrm[:], in0=x1fm[:, k * QT:(k + 1) * QT], in1=lnm[:]),
                reads=[rx1fm, rlnm], writes=[rnrm]))
            o.append(lambda: dve(lambda e: e.scalar_tensor_tensor(
                out=nrm[:], in0=tgx[:], scalar=1.0, in1=nrm[:],
                op0=ALU.add, op1=ALU.mult),
                reads=[rtgx, rnrm], writes=[rnrm]))
            o.append(lambda k=k: pool(lambda e: e.tensor_copy(
                ce[:, 0:1], carryA[:, k:k + 1]),
                reads=[rcA], writes=[rce], tag="ce0"))
            o.append(lambda k=k: dve(lambda e: e.tensor_tensor_scan(
                out=ce[:, 1:QT], data0=tga[:, 0:QT - 1], data1=tga[:, 0:QT - 1],
                initial=carryA[:, k:k + 1], op0=ALU.add, op1=ALU.bypass),
                reads=[rtga, rcA], writes=[rce], tag="scanA"))
            o.append(lambda k=k: pool(lambda e: e.tensor_add(
                out=carryA[:, k:k + 1], in0=ce[:, QT - 1:QT], in1=tga[:, QT - 1:QT]),
                reads=[rce, rtga], writes=[rcA], tag="cAup"))
            o.append(lambda: act(lambda e: e.activation(
                out=p_[:], in_=ce[:], func=AF.Exp),
                reads=[rce], writes=[rp], tag="expp"))
            o.append(lambda: dve(lambda e: e.tensor_scalar_max(
                out=p_[:], in0=p_[:], scalar1=SCAN_EPS),
                reads=[rp], writes=[rp]))
            o.append(lambda: dve(lambda e: e.tensor_scalar_max(
                out=ce[:], in0=ce[:], scalar1=-C7U),
                reads=[rce], writes=[rce], tag="cecl"))
            o.append(lambda: act(lambda e: e.activation(
                out=r_[:], in_=ce[:], func=AF.Exp, scale=f32(-1.0)),
                reads=[rce], writes=[rr], tag="expr"))
            o.append(lambda: dve(lambda e: e.tensor_scalar_min(
                out=r_[:], in0=r_[:], scalar1=1.0 / SCAN_EPS),
                reads=[rr], writes=[rr]))
            o.append(lambda: dve(lambda e: e.scalar_tensor_tensor(
                out=nrm[:], in0=nrm[:], scalar=SCAN_EPS, in1=r_[:],
                op0=ALU.add, op1=ALU.mult),
                reads=[rnrm, rr], writes=[rnrm], tag="w"))
            o.append(lambda k=k: dve(lambda e: e.tensor_tensor_scan(
                out=S_[:], data0=nrm[:], data1=nrm[:],
                initial=carryB[:, k:k + 1], op0=ALU.add, op1=ALU.bypass),
                reads=[rnrm, rcB], writes=[rS], tag="scanB"))
            o.append(lambda k=k: pool(lambda e: e.tensor_copy(
                carryB[:, k:k + 1], S_[:, QT - 1:QT]),
                reads=[rS], writes=[rcB], tag="cBup"))
            o.append(lambda k=k: pool(lambda e: e.tensor_mul(
                out=h_fm[:, k * QT:(k + 1) * QT], in0=p_[:], in1=S_[:]),
                reads=[rp, rS], writes=["h_fm"], tag="h"))
            return o

        if stage >= 3:
            for kp in (0, 3, 6):
                grp = [chunk_ops(kp + i) for i in range(min(3, ND - kp))]
                for i in range(max(len(g) for g in grp)):
                    for g in grp:
                        if i < len(g):
                            g[i]()

        if dbg:
            dma("scalar", lambda e, q=q: e.dma_start(
                out=dbg_h.rearrange("(k p) t -> p k t", p=128)[
                    :, :, q * QT:(q + 1) * QT],
                in_=h_fm.ap().rearrange("p (k t) -> p k t", k=ND)),
                reads=["h_fm"], lane=f"dbg{q * 4}", tag="dbg_h")

        return

    def emit_A4(q):
        def a4_ops(jj):
            j = q * 4 + jj
            par = j % 2
            htok, ln2h = htok_[par], ln2h_[par]
            r2t, x2t = r2t_[par], x2t_[par]
            lgt, eqt, cum = lgt_[par], eqt_[par], cum_[par]
            rhtok, rln2h = f"htok{par}", f"x2t{par}"
            rr2t, rx2t = f"r2t{par}", f"x2t{par}"
            rlgt, reqt, rcum = f"lgt{par}", f"eqt{par}", f"cum{par}"
            lps = ps[par]
            rlga = f"ps{par}a"
            rlgb = f"ps{par}b"
            xb = xr_t[par]
            rx = f"xr{par}"
            o = []
            o.append(lambda j=j, xb=xb, rx=rx: dma(
                "sync", lambda e: e.dma_start(
                    out=xb[:], in_=x_h[j * 128:(j + 1) * 128, :]),
                reads=[rx], writes=[rx], lane=f"xr{par}", tag="xrload"))
            for k in range(ND):
                o.append(lambda k=k, jj=jj, par=par: pe(
                    lambda e: e.transpose(
                        out=ps[par][:, k * 128:(k + 1) * 128],
                        in_=h_fm[:, k * QT + jj * 128: k * QT + (jj + 1) * 128],
                        identity=identf[:]),
                    reads=["h_fm", "identf"],
                    writes=[f"ps{par}a" if k < 4 else f"ps{par}b"], tag="htr"))
            o.append(lambda par=par, htok=htok, rhtok=rhtok: dve(
                lambda e: e.tensor_copy(htok[:], ps[par][:]),
                reads=[f"ps{par}a", f"ps{par}b"], writes=[rhtok], tag="hev"))
            layernorm(htok, rhtok, ln2h, rln2h, LNS_EPS, 1 + 4 * par,
                      f"ln2_{j}", defer=o)
            o.append(lambda xb=xb, rx=rx, r2t=r2t, ln2h=ln2h, rln2h=rln2h,
                     rr2t=rr2t: dve(
                lambda e: e.tensor_add(out=r2t[:], in0=xb[:], in1=ln2h[:]),
                reads=[rx, rln2h], writes=[rr2t], tag="resid2"))
            o.append(lambda j=j, r2t=r2t, rr2t=rr2t: dma(
                "scalar", lambda e: e.dma_start(
                    out=r2_scr[j * 128:(j + 1) * 128, :], in_=r2t[:]),
                reads=[rr2t], writes=["r2d"], lane=f"spr{j % 2}", tag="spill_r2"))

            layernorm(r2t, rr2t, x2t, rx2t, LNS_EPS, 2 + 4 * par,
                      f"ln3_{j}", defer=o)
            o.append(lambda j=j, x2t=x2t, rx2t=rx2t: dma(
                "scalar", lambda e: e.dma_start(
                    out=x2_scr[j * 128:(j + 1) * 128, :], in_=x2t[:]),
                reads=[rx2t], writes=["x2d"], lane=f"spx{j % 2}", tag="spill_x2"))
            xfT = x2fmT[par]
            rxfT = f"x2fmT{par}"
            xfTb = x2fmTb[par]
            rxfTb = f"x2fmTb{par}"
            for k in range(ND):
                o.append(lambda k=k, par=par, x2t=x2t, rx2t=rx2t: pe(
                    lambda e: e.transpose(
                        out=ps[par][:, k * 128:(k + 1) * 128],
                        in_=x2t[:, k * 128:(k + 1) * 128], identity=identf[:]),
                    reads=[rx2t, "identf"],
                    writes=[f"ps{par}a" if k < 4 else f"ps{par}b"], tag="x2tr"))
            o.append(lambda xfT=xfT, par=par, rxfT=rxfT: dve(
                lambda e: e.tensor_copy(xfT[:], ps[par][:]),
                reads=[f"ps{par}a", f"ps{par}b"], writes=[rxfT], tag="x2ev"))
            o.append(lambda xfTb=xfTb, xfT=xfT, rxfT=rxfT, rxfTb=rxfTb: act(
                lambda e: e.activation(out=xfTb[:], in_=xfT[:], func=AF.Identity),
                reads=[rxfT], writes=[rxfTb], tag="x2evb"))
            o.append(lambda j=j, xfTb=xfTb, rxfTb=rxfTb: dma(
                "gpsimd", lambda e: e.dma_start(
                    out=x2fm_scr.rearrange("(k p) t -> p k t", p=128)[
                        :, :, j * 128:(j + 1) * 128],
                    in_=xfTb.ap().rearrange("p (k t) -> p k t", k=ND)),
                reads=[rxfTb], writes=["x2fmd"], lane=f"fmsp{j % 2}",
                tag="spill_fm"))
            for k in range(ND):
                o.append(lambda k=k, xfT=xfT, rxfT=rxfT, lps=lps, rlga=rlga: pe(
                    lambda e: e.matmul(
                        out=lps[:, 0:E], lhsT=xfT[:, k * 128:(k + 1) * 128],
                        rhs=wrf[:, k * E:(k + 1) * E],
                        start=(k == 0), stop=(k == ND - 1)),
                    reads=[rxfT, "wrf"], writes=[rlga], tag="lgmm"))
            o.append(lambda lgt=lgt, lps=lps, rlga=rlga, rlgt=rlgt: dve(
                lambda e: e.tensor_copy(lgt[:, 0:E], lps[:, 0:E]),
                reads=[rlga], writes=[rlgt], tag="lgev"))
            if dbg:
                o.append(lambda j=j, lgt=lgt, rlgt=rlgt: dma(
                    "scalar", lambda e: e.dma_start(
                        out=dbg_lg[j * 128:(j + 1) * 128, :], in_=lgt[:, 0:E]),
                    reads=[rlgt], lane=f"dbg{j}", tag="dbg_lg"))
            o.append(lambda lgt=lgt, rlgt=rlgt: dve(
                lambda e: e.reduce_max(out=lgt[:, 4:5], in_=lgt[:, 0:E], axis=AX.X),
                reads=[rlgt], writes=[rlgt], tag="mmax"))
            o.append(lambda lgt=lgt, eqt=eqt, rlgt=rlgt, reqt=reqt: dve(
                lambda e: e.tensor_scalar(out=eqt[:], in0=lgt[:, 0:E],
                                          scalar1=lgt[:, 4:5], scalar2=None,
                                          op0=ALU.is_equal),
                reads=[rlgt], writes=[reqt], tag="meq"))
            mc = mcols.ap()[:, j * E:(j + 1) * E]
            o.append(lambda mc=mc, eqt=eqt, reqt=reqt: dve(
                lambda e: e.tensor_copy(mc[:, 0:1], eqt[:, 0:1]),
                reads=[reqt], writes=["mcols"], tag="m0"))
            o.append(lambda cum=cum, eqt=eqt, reqt=reqt, rcum=rcum: dve(
                lambda e: e.tensor_copy(cum[:, 0:1], eqt[:, 0:1]),
                reads=[reqt], writes=[rcum]))
            for e_ in range(1, E):
                o.append(lambda cum=cum, rcum=rcum: dve(
                    lambda e: e.tensor_scalar(
                        out=cum[:, 1:2], in0=cum[:, 0:1], scalar1=-1.0,
                        scalar2=1.0, op0=ALU.mult, op1=ALU.add),
                    reads=[rcum], writes=[rcum]))
                o.append(lambda e_=e_, mc=mc, eqt=eqt, cum=cum, reqt=reqt,
                         rcum=rcum: dve(
                    lambda e: e.tensor_mul(
                        out=mc[:, e_:e_ + 1], in0=eqt[:, e_:e_ + 1],
                        in1=cum[:, 1:2]),
                    reads=[reqt, rcum], writes=["mcols"]))
                if e_ < E - 1:
                    o.append(lambda e_=e_, mc=mc, cum=cum, rcum=rcum: dve(
                        lambda e: e.tensor_add(
                            out=cum[:, 0:1], in0=cum[:, 0:1],
                            in1=mc[:, e_:e_ + 1]),
                        reads=[rcum, "mcols"], writes=[rcum]))
            if dbg:
                o.append(lambda j=j, mc=mc: dma(
                    "scalar", lambda e: e.dma_start(
                        out=dbg_m[j * 128:(j + 1) * 128, :], in_=mc),
                    reads=["mcols"], lane=f"dbg{j}", tag="dbg_m"))
            o.append(lambda mc=mc, lps=lps, rlgb=rlgb: pe(
                lambda e: e.transpose(
                    out=lps[0:4, 512:640], in_=mc, identity=identf[:]),
                reads=["mcols", "identf"], writes=[rlgb], tag="mtr"))
            m4f, m4b = m4f_[par], m4b_[par]
            rm4f, rm4b = f"m4f{par}", f"m4b{par}"
            o.append(lambda lps=lps, rlgb=rlgb, m4f=m4f, rm4f=rm4f: dve(
                lambda e: e.tensor_copy(m4f[:], lps[0:4, 512:640]),
                reads=[rlgb], writes=[rm4f], tag="m4ev"))
            o.append(lambda m4b=m4b, m4f=m4f, rm4f=rm4f, rm4b=rm4b: dve(
                lambda e: e.tensor_copy(m4b[:], m4f[:]),
                reads=[rm4f], writes=[rm4b]))
            for e_ in range(E):
                o.append(lambda e_=e_, j=j, m4b=m4b, rm4b=rm4b: dma(
                    "scalar", lambda e: e.dma_start(
                        out=m2r[e_][0:1, j * 128:(j + 1) * 128],
                        in_=m4b[e_:e_ + 1, :]),
                    reads=[rm4b], writes=[f"m2r{e_}"], lane=f"mrow{j % 2}",
                    tag="mrow"))
            o.append(lambda j=j, m4b=m4b, rm4b=rm4b: dma(
                "scalar", lambda e: e.dma_start(
                    out=m4r[0:4, j * 128:(j + 1) * 128], in_=m4b[:]),
                reads=[rm4b], writes=["m4r"], lane=f"mrow{j % 2}", tag="mrow4"))
            return o

        if stage >= 4:
            for jp in range(0, 4, 2):
                oa, ob = a4_ops(jp), a4_ops(jp + 1)
                for i in range(max(len(oa), len(ob))):
                    if i < len(oa):
                        oa[i]()
                    if i < len(ob):
                        ob[i]()

    nattn = NQ if stage >= 5 else (1 if stage >= 1.2 else 0)
    if nattn:
        emit_A1(0)
    for q in range(nattn):
        emit_chunks(q)
        if q + 1 < nattn:
            emit_A1(q + 1)
        emit_A4(q)

    S.fence()

    x2fq_r = x2fq.ap().rearrange("p (k t) -> p k t", k=ND)

    def emit_F0(q):
        dma("sync", lambda e, q=q: e.dma_start(
            out=x2fq_r,
            in_=x2fm_scr.rearrange("(k p) t -> p k t", p=128)[
                :, :, q * QT:(q + 1) * QT]),
            reads=["x2fmd"],
            writes=["x2fq"], lane="x2fq", tag="x2fq")
        for e_ in range(E):
            bank = f"ps3{'ab'[e_ % 2]}"
            off = (e_ % 2) * 512
            pe(lambda e, e_=e_, q=q, off=off: e.matmul(
                out=ps[3][:, off:off + QT], lhsT=onesb[0:1, :],
                rhs=m2r[e_][0:1, q * QT:(q + 1) * QT], start=True, stop=True),
               reads=[f"m2r{e_}", "onesb"], writes=[bank], tag="mbc")
            dve(lambda e, e_=e_, off=off: e.tensor_copy(
                maskb[e_][:], ps[3][:, off:off + QT]),
                reads=[bank], writes=[f"maskb{e_}"], tag="mbev")
        for e_ in range(E):
            for k in range(ND):
                dve(lambda e, e_=e_, k=k: e.tensor_mul(
                    out=x2fe[e_][:, k * QT:(k + 1) * QT],
                    in0=x2fq[:, k * QT:(k + 1) * QT], in1=maskb[e_][:]),
                    reads=["x2fq", f"maskb{e_}"], writes=[f"x2fe{e_}"], tag="xfe")

    def emit_Z(q):
        for ig in range(NI // 2):
            wb = w1bb[ig % 2]
            rwb = f"w1bb{ig % 2}"
            for e_ in range(E):
                src = w1b_scr[e_].rearrange("(k p) i -> p k i", p=128)[
                    :, :, ig * 256:(ig + 1) * 256]
                dst = wb.ap().rearrange("p (e k i) -> p e k i", e=E, k=ND)[:, e_]
                dma("sync",
                    lambda e, src=src, dst=dst: e.dma_start(out=dst, in_=src),
                    reads=["w1scr"],
                    writes=[f"{rwb}e{e_}"], lane=f"{rwb}e{e_}", tag="w1ld")
            for ii in range(2):
                it = ig * 2 + ii
                zbank = f"ps{(it % 8) // 2}{'ab'[it % 2]}"
                zoff = (it % 2) * 512
                zp = ps[(it % 8) // 2]
                for e_ in range(E):
                    for k in range(ND):
                        pe(lambda e, e_=e_, k=k, ii=ii, wb=wb, zp=zp, zoff=zoff:
                           e.matmul(
                            out=zp[:, zoff:zoff + QT],
                            lhsT=wb[:, ((e_ * ND + k) * 256) + ii * 128:
                                    ((e_ * ND + k) * 256) + ii * 128 + 128],
                            rhs=x2fe[e_][:, k * QT:(k + 1) * QT],
                            start=(e_ == 0 and k == 0), stop=False),
                           reads=[f"{rwb}e{e_}", f"x2fe{e_}"], writes=[zbank],
                           tag="zmm")
                pe(lambda e, it=it, q=q, zp=zp, zoff=zoff: e.matmul(
                    out=zp[:, zoff:zoff + QT],
                    lhsT=b1b4[0:4, it * 128:(it + 1) * 128],
                    rhs=m4r[0:4, q * QT:(q + 1) * QT],
                    start=False, stop=True),
                   reads=["b1b4", "m4r"], writes=[zbank], tag="zbias")
                act(lambda e, it=it, zp=zp, zoff=zoff: e.activation(
                    out=interm[:, it * QT:(it + 1) * QT],
                    in_=zp[:, zoff:zoff + QT], func=AF.Gelu),
                    reads=[zbank], writes=[f"interm{it}"], tag="gelu")

    nffn = 0 if stage < 6 else (NQ if stage >= 7 else 1)
    if nffn:
        emit_F0(0)
    for q in range(nffn):
        emit_Z(q)
        if q + 1 < nffn:
            emit_F0(q + 1)
        for ic in range(NI if stage >= 7 else 0):
            wo_b = wobb[ic % 2]
            rwo = f"wobb{ic % 2}"
            dma("sync", lambda e, ic=ic, wo_b=wo_b: e.dma_start(
                out=wo_b[:], in_=wob_scr[ic * 128:(ic + 1) * 128, :]),
                reads=["woscr"], writes=[rwo], lane=rwo, tag="wold")
            for tt in range(4):
                for half in range(2):
                    pe(lambda e, ic=ic, tt=tt, half=half, wo_b=wo_b: e.matmul(
                        out=ps[tt][:, half * 512:(half + 1) * 512],
                        lhsT=interm[:, ic * QT + tt * 128: ic * QT + (tt + 1) * 128],
                        rhs=wo_b[:, half * 512:(half + 1) * 512],
                        start=(ic == 0), stop=(ic == NI - 1)),
                       reads=[rwo, f"interm{ic}"],
                       writes=[f"ps{tt}{'ab'[half]}"], tag="ymm")

        for tt in range(4 if stage >= 7 else 0):
            j = q * 4 + tt
            x2r = x2rel[tt % 2]
            rx2r = f"x2rel{tt % 2}"
            r2r = r2rel[tt % 2]
            rr2r = f"r2rel{tt % 2}"
            dma("scalar", lambda e, j=j, x2r=x2r: e.dma_start(
                out=x2r[:], in_=x2_scr[j * 128:(j + 1) * 128, :]),
                reads=["x2d"], writes=[rx2r], lane=rx2r, tag="x2rel")
            dma("scalar", lambda e, j=j, r2r=r2r: e.dma_start(
                out=r2r[:], in_=r2_scr[j * 128:(j + 1) * 128, :]),
                reads=["r2d"], writes=[rr2r], lane=rr2r, tag="r2rel")
            dve(lambda e, tt=tt, x2r=x2r: e.tensor_add(
                out=ubuf[:], in0=ps[tt][:], in1=x2r[:]),
                reads=[f"ps{tt}a", f"ps{tt}b", rx2r], writes=["ubuf"], tag="yadd")
            dve(lambda e: e.tensor_add(out=ubuf[:], in0=ubuf[:], in1=bo_b[:]),
                reads=["ubuf", "o_b"], writes=["ubuf"], tag="boadd")
            layernorm(ubuf, "ubuf", y2buf, "y2buf", LN_EPS, 3, f"lni_{j}")
            pool(lambda e: e.tensor_mul(out=y2buf[:], in0=y2buf[:], in1=gam_b[:]),
                 reads=["y2buf", "g_b"], writes=["y2buf"], tag="gmul")
            pool(lambda e: e.tensor_add(out=y2buf[:], in0=y2buf[:], in1=bet_b[:]),
                 reads=["y2buf", "b_b"], writes=["y2buf"], tag="badd")
            if dbg:
                dma("scalar", lambda e, j=j: e.dma_start(
                    out=dbg_u2[j * 128:(j + 1) * 128, :], in_=y2buf[:]),
                    reads=["y2buf"], lane=f"dbg{j}", tag="dbg_u2")
            layernorm(y2buf, "y2buf", ubuf, "ubuf", LNS_EPS, 0, f"lns_{j}")
            dve(lambda e, r2r=r2r: e.tensor_add(out=ubuf[:], in0=ubuf[:], in1=r2r[:]),
                reads=["ubuf", rr2r], writes=["ubuf"], tag="fadd")
            dma("scalar", lambda e, j=j: e.dma_start(
                out=out_h[j * 128:(j + 1) * 128, :], in_=ubuf[:]),
                reads=["ubuf"], writes=[f"out{j}"], lane=f"out{j % 2}", tag="outdma")

    if stage >= 7:
        S.add("sync", lambda e: e.nop(),
              deps=[S.lw[f"out{j}"] for j in range(NT)], tag="final_wait")
    else:
        S.fence(tag="trunc_fence")
        od = dma("scalar", lambda e: e.dma_start(
            out=out_h[0:128, 0:128], in_=identf[:]), reads=["identf"],
            lane="out0", tag="trunc_out")
        S.add("sync", lambda e: e.nop(), deps=[od], tag="final_wait")

    A_per.allocate()
    A_attn.allocate()
    A_attn.free()
    A_ffn.allocate()
    S.finalize(nc, es)
    with nc.Block() as block:
        @block.sync
        def _(eng):
            S.replay("sync", eng)

        @block.gpsimd
        def _(eng):
            S.replay("gpsimd", eng)

        @block.vector
        def _(eng):
            S.replay("vector", eng)

        @block.scalar
        def _(eng):
            S.replay("scalar", eng)

        @block.tensor
        def _(eng):
            S.replay("tensor", eng)

    es.close()
    return nc


_NC_CACHE = {}


def kernel(L0_attn_x, L0_attn_a_param, L0_attn_w_in, L0_attn_b_in, L0_attn_w_a,
           L0_attn_b_a, L0_ffn_W_router, L0_ffn_W1, L0_ffn_b1, L0_ffn_Wo,
           L0_ffn_bo, L0_ffn_ln_gamma, L0_ffn_ln_beta, _dbg=False):
    if _dbg not in _NC_CACHE:
        _NC_CACHE[_dbg] = build(dbg=_dbg)
    nc = _NC_CACHE[_dbg]
    c = np.ascontiguousarray
    shared = dict(
        a_param=c(L0_attn_a_param, dtype=np.float32),
        w_in=c(L0_attn_w_in, dtype=np.float32),
        b_in=c(L0_attn_b_in, dtype=np.float32),
        w_a=c(L0_attn_w_a, dtype=np.float32),
        b_a=c(L0_attn_b_a, dtype=np.float32),
        w_router=c(L0_ffn_W_router, dtype=np.float32),
        w1=c(L0_ffn_W1, dtype=np.float32),
        b1=c(L0_ffn_b1, dtype=np.float32),
        wo=c(L0_ffn_Wo, dtype=np.float32),
        bo=c(L0_ffn_bo, dtype=np.float32),
        ln_gamma=c(L0_ffn_ln_gamma, dtype=np.float32),
        ln_beta=c(L0_ffn_ln_beta, dtype=np.float32),
    )
    in_maps = [dict(shared, x=c(L0_attn_x[b], dtype=np.float32)) for b in range(8)]
    res = run_bass_kernel_spmd(nc, in_maps, core_ids=list(range(8)))
    out = np.stack([np.asarray(res.results[b]["out"], dtype=np.float32)
                    for b in range(8)], axis=0)
    if _dbg:
        return out, res
    return out
